# version 28
# baseline (speedup 1.0000x reference)
import contextlib
import numpy as np
import ml_dtypes
import concourse.bass as bass
import concourse.mybir as mybir
from concourse.bass_utils import run_bass_kernel_spmd

F32 = mybir.dt.float32
BF16 = mybir.dt.bfloat16
I32 = mybir.dt.int32
AF = mybir.ActivationFunctionType
ALU = mybir.AluOpType
AX = mybir.AxisListType
BF = ml_dtypes.bfloat16

S, DM, HID = 8192, 1024, 3072
NS = 16
QR = NS * 128
DC = DM // 128
HC = HID // 128
SCALE = 1.0 / 127.0
EPS = 1e-5
NEG = -30000.0
SELF_SYNC = set()
DBG_STATS = False
FAST_OK = set()


class Emit:

    def __init__(self, nc):
        self.nc = nc
        self.es = contextlib.ExitStack()
        self.sem = {}
        self.cnt = {}
        self.qs = {k: [] for k in ("sy", "pl", "v", "a", "t")}
        for name in ("v", "a", "t"):
            self._mk(name)

    def _mk(self, name):
        if name not in self.sem:
            self.sem[name] = self.es.enter_context(
                self.nc.semaphore("sem_" + name))
            self.cnt[name] = 0
        return self.sem[name]

    def qwait(self, ek, semkey, val=None):
        self._mk(semkey)
        v = self.cnt[semkey] if val is None else val
        self.qs[ek].append(("w", self.sem[semkey], v))

    def qop(self, ek, fn, inckey=None):
        mark = None
        if inckey is not None:
            self._mk(inckey)
            self.cnt[inckey] += 1
            mark = self.cnt[inckey]
            self.qs[ek].append(("o", fn, self.sem[inckey], 1))
        else:
            self.qs[ek].append(("o", fn, None, 0))
        return mark

    def qsop(self, ek, fn=None, fast=False, dep=None):
        key = {"v": "v", "a": "a", "t": "t"}[ek]
        if key == "v" and (fast is False or fast not in FAST_OK):
            if not (isinstance(dep, int) and dep < 0):
                self.qwait(ek, key, dep)
        return self.qop(ek, fn, key)

    def qdma(self, ek, semkey, fn):
        self._mk(semkey)
        self.cnt[semkey] += 16
        mark = self.cnt[semkey]
        self.qs[ek].append(("o", fn, self.sem[semkey], 16))
        return mark

    def qgroup(self, ek, fns, inckey):
        self._mk(inckey)
        self.cnt[inckey] += 1
        mark = self.cnt[inckey]
        self.qs[ek].append(("g", fns, self.sem[inckey], 1))
        return mark

    def mark(self, key):
        self._mk(key)
        return self.cnt[key]

    def barrier(self):
        marks = dict(self.cnt)
        for ek in self.qs:
            for kk, vv in marks.items():
                self.qs[ek].append(("w", self.sem[kk], vv))

    def flush(self, blk):
        def emit(eng, items):
            for it in items:
                if it[0] == "w":
                    eng.wait_ge(it[1], it[2])
                elif it[0] == "o":
                    ins = it[1](eng)
                    if it[2] is not None:
                        ins.then_inc(it[2], it[3])
                else:
                    for fn in it[1]:
                        ins = fn(eng)
                    ins.then_inc(it[2], it[3])

        @blk.sync
        def _(s):
            emit(s, self.qs["sy"])

        @blk.gpsimd
        def _(g):
            emit(g, self.qs["pl"])

        @blk.vector
        def _(v):
            emit(v, self.qs["v"])

        @blk.scalar
        def _(a):
            emit(a, self.qs["a"])

        @blk.tensor
        def _(t):
            emit(t, self.qs["t"])


def build(nphase=9):
    nc = bass.Bass("TRN2", target_bir_lowering=False, debug=False)

    xk_h = nc.dram_tensor("xk", [S, DM], F32, kind="ExternalInput").ap()
    xq_h = nc.dram_tensor("xq", [QR, DM], F32, kind="ExternalInput").ap()
    wq_h = nc.dram_tensor("wq", [DM, DM], F32, kind="ExternalInput").ap()
    wk_h = nc.dram_tensor("wk", [DM, DM], F32, kind="ExternalInput").ap()
    wv_h = nc.dram_tensor("wv", [DM, DM], F32, kind="ExternalInput").ap()
    wg_h = nc.dram_tensor("wg", [HID, DM], I32, kind="ExternalInput").ap()
    wu_h = nc.dram_tensor("wu", [HID, DM], I32, kind="ExternalInput").ap()
    wd_h = nc.dram_tensor("wd", [DM, HID], I32, kind="ExternalInput").ap()
    mask_h = nc.dram_tensor("mask", [128, 512], F32, kind="ExternalInput").ap()
    ident_h = nc.dram_tensor("ident", [128, 128], BF16, kind="ExternalInput").ap()
    out_h = nc.dram_tensor("out", [QR, DM], F32, kind="ExternalOutput").ap()

    kT_s = nc.dram_tensor("kT_s", [S // 512, 128, DC * 512], BF16,
                          kind="Internal").ap()
    vv_s = nc.dram_tensor("vv_s", [S // 512, 128, 4 * DM], BF16,
                          kind="Internal").ap()
    wgT_s = nc.dram_tensor("wgT_s", [DM, HID], BF16, kind="Internal").ap()
    wuT_s = nc.dram_tensor("wuT_s", [DM, HID], BF16, kind="Internal").ap()
    wdT_s = nc.dram_tensor("wdT_s", [HID, DM], BF16, kind="Internal").ap()

    BASE = 16896

    def at(name, shape, dtype, off):
        return nc.alloc_sbuf_tensor_at(name, shape, dtype, offset=BASE + off)

    wq_b = at("wq_b", [128, DC, DM], BF16, 0)
    wk_b = at("wk_b", [128, DC, DM], BF16, 16384)
    wv_b = at("wv_b", [128, DC, DM], BF16, 32768)
    xnq = at("xnq", [128, DC, QR], BF16, 49152)
    qT = at("qT", [128, DC, QR], BF16, 81920)
    g_b = at("g_b", [128, NS, DM], BF16, 114688)
    ident = at("ident_b", [128, 128], BF16, 147456)
    maskb = at("maskb", [128, 512], F32, 147712)
    bn6 = at("bn6", [128, 2, 12], F32, 149760)
    msb = at("msb", [128, 2, 16], F32, 149824)
    lsb = at("lsb", [128, 2, 16], F32, 149952)
    fsb = at("fsb", [128, 2, 16], F32, 150080)
    mglob = at("mglob", [128, 2], F32, 150208)
    lsum = at("lsum", [128, 2], F32, 150240)
    linv = at("linv", [128, 2], F32, 150272)
    veps = at("veps", [128, 80], F32, 150304)
    STG = 152064

    wst = at("wst", [128, HC, DM], BF16, 81920)
    wts = at("wts", [128, 4, 1536], BF16, 131072)
    xkb4 = at("xkb4", [128, 2, 4, DM], F32, STG)
    xnr = at("xnr", [128, 2, DM], BF16, STG + 32768)
    kst = at("kst", [128, DC, 512], BF16, STG + 36864)
    vst = at("vst", [128, 4, DM], BF16, STG + 45056)
    xns = at("xns", [128, 2, DC, 512], BF16, 49152)
    mvall = at("mvall", [128, 80, 2], F32, 150656)
    rstdall = at("rstdall", [128, 80], F32, 151296)
    wdT_b = at("wdT_b", [128, HC, DM], BF16, 0)
    hact = at("hact", [128, HC, DM], BF16, STG)
    wgu = at("wgu", [128, 2, 2, DC, 128], BF16, STG + 49152)
    silu_t = at("silu_t", [128, DM], BF16, STG + 57344)
    oacc = at("oacc", [128, NS, DM], F32, 0)
    pbuf5 = at("pbuf5", [128, 4, 2, 512], BF16, 65536)
    xrs = at("xrs", [128, DM], F32, 73728)
    sg_t = at("sg_t", [128, DM], F32, 77824)
    kbuf = at("kbuf", [128, 2, DC, 512], BF16, STG)
    kbufX = at("kbufX", [128, 2, DC, 512], BF16, STG + 16384)
    vbuf = at("vbuf", [128, 2, 4, DM], BF16, STG + 32768)
    pT = at("pT", [128, 2, 2, 4, 128], BF16, STG + 49152)
    outt = at("outt", [128, DM], F32, STG + 53248)
    lsb16 = at("lsb16", [128, NS, 16], F32, 150656)

    psA = nc.alloc_psum_tensor("psA", [128, 2048], F32)
    psBC = nc.alloc_psum_tensor("psBC", [128, 1024], F32)
    psD = nc.alloc_psum_tensor("psD", [128, 1024], F32)
    trA = psA.bitcast(BF16)
    trBC = psBC.bitcast(BF16)

    E = Emit(nc)

    for w_h, w_b in ((wk_h, wk_b), (wv_h, wv_b), (wq_h, wq_b)):
        E.qdma("pl", "wqkv", lambda g, w_h=w_h, w_b=w_b: g.dma_start(
            out=w_b[:], in_=w_h.rearrange("(c p) e -> p c e", p=128)))
    E.qdma("sy", "const", lambda s: s.dma_start(out=ident[:], in_=ident_h[:]))
    E.qdma("sy", "const", lambda s: s.dma_start(out=maskb[:], in_=mask_h[:]))
    sp_const = E.mark("const")

    rot = []
    wtasks = []
    wst_m = {}
    half_trm = {}
    wts_dma = []
    wts_final = {}
    wstf = wst.ap().rearrange("p a b -> p (a b)")
    halves = [(wi, hf) for wi in range(3) for hf in range(2)]

    def _half_params(h):
        wi, hf = halves[h]
        is_d = wi == 2
        nrow = DC if is_d else HC
        ncol = HC if is_d else DC
        wfree = HID if is_d else DM
        hr = nrow // 2
        return wi, hf, ncol, wfree, hr, hf * hr

    def issue_half_load(h):
        wi, hf, ncol, wfree, hr, r_lo = _half_params(h)
        w_h = (wg_h, wu_h, wd_h)[wi]
        hpar = h % 2
        qn = "pl"
        if h >= 2:
            E.qwait(qn, "t", half_trm[h - 2])
        E.qdma(qn, "wst%d" % hpar, lambda g, w_h=w_h, r_lo=r_lo, hr=hr,
               wfree=wfree, hpar=hpar: g.dma_start(
                   out=wstf[:, hpar * 12288:
                            hpar * 12288 + hr * wfree].rearrange(
                       "p (a b) -> p a b", b=wfree),
                   in_=w_h[r_lo * 128:(r_lo + hr) * 128, :].rearrange(
                       "(c p) e -> p c e", p=128)))
        wst_m[h] = E.mark("wst%d" % hpar)

    if nphase >= 1:
        issue_half_load(0)
        issue_half_load(1)
        for h in range(6):
            wi, hf, ncol, wfree, hr, r_lo = _half_params(h)
            nunit = (hr + 7) // 8
            for cb in range(ncol):
                for u in range(nunit):
                    r0, r1 = u * 8, min(hr, u * 8 + 8)
                    wtasks.append(("g", h, cb, r0, r1, wfree, r_lo, hr))
                wtasks.append(("d", h, cb, hr, r_lo))
            wtasks.append(("l", h))

    pump_on = [False]

    def pump(n):
        wT_tab = (wgT_s, wuT_s, wdT_s)
        while n > 0 and wtasks:
            task = wtasks.pop(0)
            if task[0] == "g":
                _, h, cb, r0, r1, wfree, r_lo, hr = task
                hpar = h % 2
                rb = len(rot) % 2
                E.qwait("t", "const", sp_const)
                E.qwait("t", "wst%d" % hpar, wst_m[h])
                if len(rot) >= 2:
                    E.qwait("t", "a", rot[-2])
                half_trm[h] = E.qgroup("t", [
                    (lambda t, r=r, r0=r0, cb=cb, rb=rb,
                     wfree=wfree, hpar=hpar: t.transpose(
                         trA[:, rb * 1024 + (r - r0) * 128:
                             rb * 1024 + (r - r0 + 1) * 128],
                         wstf[:, hpar * 12288 + r * wfree + cb * 128:
                              hpar * 12288 + r * wfree + (cb + 1) * 128],
                         ident[:]))
                    for r in range(r0, r1)], "t")
                E.qwait("a", "t")
                if len(wts_dma) >= 4:
                    E.qwait("a", *wts_dma[-4])
                rot.append(E.qsop("a", lambda a, r0=r0, r1=r1, rb=rb,
                                  wpar=cb % 4: a.activation(
                    out=wts[:, wpar, r0 * 128:r1 * 128],
                    in_=trA[:, rb * 1024:
                            rb * 1024 + (r1 - r0) * 128],
                    func=AF.Copy)))
                n -= 1
            elif task[0] == "d":
                _, h, cb, hr, r_lo = task
                wT_s = wT_tab[halves[h][0]]
                E.qwait("a", "a")
                wts_dma.append(("wTs%d" % (cb % 4), E.qdma(
                    "a", "wTs%d" % (cb % 4), lambda a, cb=cb, hr=hr,
                    r_lo=r_lo, wT_s=wT_s, wpar=cb % 4: a.dma_start(
                        out=wT_s[cb * 128:(cb + 1) * 128,
                                 r_lo * 128:(r_lo + hr) * 128],
                        in_=wts[:, wpar, 0:hr * 128]))))
            else:
                _, h = task
                if h + 2 < 6:
                    issue_half_load(h + 2)
        if not wtasks and not wts_final:
            for kk2 in ("wTs0", "wTs1", "wTs2", "wTs3"):
                wts_final[kk2] = E.mark(kk2)

    wgT_sv = wgT_s.rearrange("(c p) h -> p c h", p=128)
    wuT_sv = wuT_s.rearrange("(c p) h -> p c h", p=128)
    wgu_pref = {}

    def issue_wgu(hb):
        par = hb % 2
        E.qdma("sy", "wgu%d" % par, lambda s, hb=hb, par=par: s.dma_start(
            out=wgu[:, par, 0, :, :],
            in_=wgT_sv[:, :, hb * 128:(hb + 1) * 128]))
        E.qdma("sy", "wgu%d" % par, lambda s, hb=hb, par=par: s.dma_start(
            out=wgu[:, par, 1, :, :],
            in_=wuT_sv[:, :, hb * 128:(hb + 1) * 128]))
        return E.mark("wgu%d" % par)

    if nphase >= 2:
        nq_tiles = 16 if nphase >= 3 else 0

        aggr_m = []

        nmark = []
        entries = [(xk_h, b, True) for b in range(16)]
        if nq_tiles:
            entries += [(xq_h, b, False) for b in range(4)]
        NB = len(entries)

        tr_tm = []
        k_evac = []
        v_evac = []
        kv_tmark = []
        rstd_ms = {}
        ld_ms = {}

        def emit_load(gb):
            src_h, lb, kv = entries[gb]
            bpar = gb % 2
            if gb >= 2:
                E.qwait("sy", "v", nmark[-1])
            E.qdma("sy", "xkb4_%d" % bpar,
                   lambda s, lb=lb, bpar=bpar, src_h=src_h: s.dma_start(
                       out=xkb4[:, bpar, :, :],
                       in_=src_h[lb * 512:(lb + 1) * 512, :].rearrange(
                           "(ti p) d -> p ti d", p=128)))
            ld_ms[gb] = E.mark("xkb4_%d" % bpar)

        def emit_stats(gb):
            bpar = gb % 2
            g4 = 4 * gb
            E.qwait("v", "xkb4_%d" % bpar, ld_ms[gb])
            for ti in range(4):
                b2 = ti % 2
                E.qsop("v", lambda v, bpar=bpar, ti=ti, b2=b2: v.bn_stats(
                    bn6[:, b2, 0:6], xkb4[:, bpar, ti, 0:512]),
                    dep=(aggr_m[-2] if len(aggr_m) >= 2 else 0))
                E.qsop("v", lambda v, bpar=bpar, ti=ti, b2=b2: v.bn_stats(
                    bn6[:, b2, 6:12], xkb4[:, bpar, ti, 512:1024]),
                    dep=-10**9)
                aggr_m.append(E.qsop("v", lambda v, g4=g4, ti=ti, b2=b2:
                                     v.bn_aggr(
                    mvall[:, g4 + ti, :], bn6[:, b2, :])))
            E.qsop("v", lambda v, g4=g4: v.tensor_scalar_add(
                veps[:, g4:g4 + 4], mvall[:, g4:g4 + 4, 1:2], EPS))
            E.qsop("v", lambda v, g4=g4: v.tensor_scalar(
                out=rstdall[:, g4:g4 + 4], in0=veps[:, g4:g4 + 4],
                scalar1=0.5, scalar2=0.5, op0=ALU.mult, op1=ALU.add))
            E.qsop("v", lambda v, g4=g4: v.reciprocal(
                rstdall[:, g4:g4 + 4], rstdall[:, g4:g4 + 4]))
            for _ in range(5):
                E.qsop("v", lambda v, g4=g4: v.tensor_mul(
                    fsb[:, 0, 0:4], rstdall[:, g4:g4 + 4],
                    rstdall[:, g4:g4 + 4]))
                E.qsop("v", lambda v, g4=g4: v.tensor_mul(
                    fsb[:, 0, 0:4], fsb[:, 0, 0:4], veps[:, g4:g4 + 4]))
                E.qsop("v", lambda v: v.tensor_scalar(
                    out=fsb[:, 0, 0:4], in0=fsb[:, 0, 0:4],
                    scalar1=-0.5, scalar2=1.5, op0=ALU.mult, op1=ALU.add))
                E.qsop("v", lambda v, g4=g4: v.tensor_mul(
                    rstdall[:, g4:g4 + 4], rstdall[:, g4:g4 + 4],
                    fsb[:, 0, 0:4]))
            rstd_ms[gb] = E.mark("v")

        xnsm = {}

        def emit_ladder(gb):
            src_h, lb, kv = entries[gb]
            bpar = gb % 2
            pb = gb % 2
            g4 = 4 * gb
            for ti in range(4):
                tile = 4 * lb + ti
                par = ti % 2
                if len(tr_tm) >= 2:
                    E.qwait("v", "t", tr_tm[-2])
                v_m = E.qsop("v", dep=rstd_ms[gb], fn=lambda v,
                             bpar=bpar, ti=ti, par=par,
                             g4=g4: v.tensor_scalar(
                    out=xnr[:, par, :], in0=xkb4[:, bpar, ti, :],
                    scalar1=mvall[:, g4 + ti, 0:1],
                    scalar2=rstdall[:, g4 + ti:g4 + ti + 1],
                    op0=ALU.subtract, op1=ALU.mult))

                if ti == 3:
                    nmark.append(v_m)
                E.qwait("t", "v", v_m)
                rb = len(rot) % 2
                if len(rot) >= 2:
                    E.qwait("t", "a", rot[-2])
                tr_tm.append(E.qgroup("t", [
                    (lambda t, dcc=dcc, par=par, rb=rb: t.transpose(
                        trA[:, rb * 1024 + dcc * 128:
                            rb * 1024 + (dcc + 1) * 128],
                        xnr[:, par, dcc * 128:(dcc + 1) * 128],
                        ident[:]))
                    for dcc in range(DC)], "t"))

                E.qwait("a", "t")
                if kv:
                    fn = (lambda a, pb=pb, ti=ti, rb=rb: a.activation(
                        out=xns[:, pb, :, ti * 128:(ti + 1) * 128],
                        in_=trA[:, rb * 1024:(rb + 1) * 1024]
                        .rearrange("p (c q) -> p c q", q=128),
                        func=AF.Copy))
                else:
                    fn = (lambda a, tile=tile, rb=rb: a.activation(
                        out=xnq[:, :, tile * 128:(tile + 1) * 128],
                        in_=trA[:, rb * 1024:(rb + 1) * 1024]
                        .rearrange("p (c q) -> p c q", q=128),
                        func=AF.Copy))
                rot.append(E.qsop("a", fn))
            xnsm[gb] = E.mark("a")

        def emit_K(gb):
            src_h, lb, kv = entries[gb]
            pb = gb % 2
            a_xns = xnsm[gb]
            for e in range(DC):
                ph = e % 2
                if pump_on[0]:
                    pump(1)
                E.qwait("t", "a", a_xns)
                if len(k_evac) >= 2:
                    E.qwait("t", "a", k_evac[-2])
                kv_tmark.append(E.qgroup("t", [
                    (lambda t, dcc=dcc, pb=pb, e=e, ph=ph: t.matmul(
                        psBC[:, ph * 512:(ph + 1) * 512],
                        wk_b[:, dcc, e * 128:(e + 1) * 128],
                        xns[:, pb, dcc, :],
                        start=(dcc == 0), stop=(dcc == DC - 1)))
                    for dcc in range(DC)], "t"))

                E.qwait("a", "t", kv_tmark[-1])
                E.qwait("a", "kst")
                k_evac.append(E.qsop("a", lambda a, e=e, ph=ph:
                                     a.activation(
                    out=kst[:, e, :],
                    in_=psBC[:, ph * 512:(ph + 1) * 512],
                    func=AF.Copy)))
            E.qwait("pl", "a")
            E.qdma("pl", "kst", lambda g, lb=lb: g.dma_start(
                out=kT_s[lb],
                in_=kst.ap().rearrange("p c q -> p (c q)")))

        def emit_V(gb):
            src_h, lb, kv = entries[gb]
            pb = gb % 2
            for sti in range(4):
                if pump_on[0]:
                    pump(2)
                vtgt = psD[:, 0:1024] if sti % 2 == 0                     else psA[:, 1024:2048]
                if len(v_evac) >= 2:
                    E.qwait("t", "a", v_evac[-2])
                fns = []
                for dcc in range(DC):
                    for eh in range(2):
                        fns.append(lambda t, dcc=dcc, eh=eh, pb=pb,
                                   sti=sti, vtgt=vtgt: t.matmul(
                            vtgt[:, eh * 512:(eh + 1) * 512],
                            xns[:, pb, dcc,
                                sti * 128:(sti + 1) * 128],
                            wv_b[:, dcc, eh * 512:(eh + 1) * 512],
                            start=(dcc == 0), stop=(dcc == DC - 1)))
                kv_tmark.append(E.qgroup("t", fns, "t"))

                E.qwait("a", "t", kv_tmark[-1])
                E.qwait("a", "vst")
                v_evac.append(E.qsop("a", lambda a, sti=sti,
                                     vtgt=vtgt: a.activation(
                    out=vst[:, sti, :], in_=vtgt, func=AF.Copy)))

            E.qwait("pl", "a")
            E.qdma("pl", "vst", lambda g, lb=lb: g.dma_start(
                out=vv_s[lb],
                in_=vst.ap().rearrange("p t e -> p (t e)")))

        def emit_Q(sb):
            for e in range(DC):
                ph = e % 2
                if pump_on[0]:
                    pump(1)
                E.qwait("t", "a", xnsm[16 + sb])
                if len(k_evac) >= 2:
                    E.qwait("t", "a", k_evac[-2])
                kv_tmark.append(E.qgroup("t", [
                    (lambda t, dcc=dcc, sb=sb, e=e, ph=ph: t.matmul(
                        psBC[:, ph * 512:(ph + 1) * 512],
                        wq_b[:, dcc, e * 128:(e + 1) * 128],
                        xnq[:, dcc, sb * 512:(sb + 1) * 512],
                        start=(dcc == 0), stop=(dcc == DC - 1)))
                    for dcc in range(DC)], "t"))
                E.qwait("a", "t", kv_tmark[-1])
                k_evac.append(E.qsop("a", lambda a, sb=sb, e=e, ph=ph:
                                     a.activation(
                    out=qT[:, e, sb * 512:(sb + 1) * 512],
                    in_=psBC[:, ph * 512:(ph + 1) * 512],
                    func=AF.Copy)))

        E.qwait("t", "wqkv")
        emit_load(0)
        emit_stats(0)
        emit_load(1)
        emit_stats(1)
        emit_ladder(0)
        emit_load(2)
        for gb in range(NB):
            kv = entries[gb][2]
            boundary = kv and gb + 1 < NB and not entries[gb + 1][2]
            if gb == 2:
                pump_on[0] = True
            if kv:
                emit_K(gb)
            elif nphase >= 4:
                emit_Q(gb - 16)
            if not boundary:
                if gb + 1 < NB:
                    emit_ladder(gb + 1)
                if gb + 3 < NB:
                    emit_load(gb + 3)
                if gb + 2 < NB:
                    emit_stats(gb + 2)
            if kv:
                emit_V(gb)
            if boundary:
                emit_ladder(gb + 1)
                if gb + 3 < NB:
                    emit_load(gb + 3)
                if gb + 2 < NB:
                    emit_stats(gb + 2)
        pump(10**6)

        if nphase >= 5:
            for kk2 in ("wTs0", "wTs1", "wTs2", "wTs3"):
                E.qwait("sy", kk2, wts_final[kk2])
            wgu_pref[(0, 1)] = issue_wgu(1)
            E.qwait("sy", "vst")
            wgu_pref[(0, 0)] = issue_wgu(0)

    E.barrier()

    E.barrier()

    if nphase >= 5:
        E.qdma("sy", "wdT", lambda s: s.dma_start(
            out=wdT_b[:], in_=wdT_s.rearrange("(c p) e -> p c e", p=128)))
        sp_wd = E.mark("wdT")

        gu_tmark = []
        hm_vmark = []
        si_amark = []
        g_amark = []
        dn_tmark = []
        for q in range(2):
            qs = q * 1024
            for hb in range(HC):
                par = hb % 2
                if (q, hb) in wgu_pref:
                    sp_m = wgu_pref.pop((q, hb))
                else:
                    if len(gu_tmark) >= 2:
                        E.qwait("sy", "t", gu_tmark[-2][1])
                    sp_m = issue_wgu(hb)

                gb = (hb % 2) * 1024
                upD = hb % 2 == 1
                E.qwait("t", "wgu%d" % par, sp_m)
                if len(si_amark) >= 2:
                    E.qwait("t", "a", si_amark[-2])
                if len(hm_vmark) >= 2:
                    E.qwait("t", "v", hm_vmark[-2])
                if upD and g_amark:
                    E.qwait("t", "a", g_amark[-1])
                fns = []
                for dcc in range(DC):
                    for sh in range(2):
                        fns.append(lambda t, dcc=dcc, sh=sh, qs=qs, par=par,
                                   gb=gb: t.matmul(
                            psA[:, gb + sh * 512:gb + (sh + 1) * 512],
                            wgu[:, par, 0, dcc, :],
                            xnq[:, dcc, qs + sh * 512:qs + (sh + 1) * 512],
                            start=(dcc == 0), stop=(dcc == DC - 1)))
                m_g = E.qgroup("t", fns, "t")
                fns = []
                for dcc in range(DC):
                    for sh in range(2):
                        fns.append(lambda t, dcc=dcc, sh=sh, qs=qs, par=par,
                                   upD=upD: t.matmul(
                            (psD if upD else psBC)[:, sh * 512:
                                                   (sh + 1) * 512],
                            wgu[:, par, 1, dcc, :],
                            xnq[:, dcc, qs + sh * 512:qs + (sh + 1) * 512],
                            start=(dcc == 0), stop=(dcc == DC - 1)))
                m_u = E.qgroup("t", fns, "t")
                gu_tmark.append((m_g, m_u))

                E.qwait("a", "t", m_g)
                if hm_vmark:
                    E.qwait("a", "v", hm_vmark[-1])
                si_amark.append(E.qsop("a", lambda a, gb=gb: a.activation(
                    out=silu_t[:], in_=psA[:, gb:gb + 1024],
                    func=AF.Silu, scale=SCALE)))

                E.qwait("v", "t", m_u)
                E.qwait("v", "a", si_amark[-1])
                hm_vmark.append(E.qsop("v", lambda v, hb=hb, upD=upD:
                                       v.tensor_tensor(
                    hact[:, hb, :], silu_t[:], (psD if upD else psBC)[:],
                    ALU.mult)))
            v_hact = E.mark("v")

            for sti in range(8):
                E.qwait("t", "v", v_hact)
                E.qwait("t", "wdT", sp_wd)
                if len(g_amark) >= 1:
                    E.qwait("t", "a", g_amark[-1])
                fns = []
                for hb in range(HC):
                    for eh in range(2):
                        fns.append(lambda t, hb=hb, sti=sti, eh=eh: t.matmul(
                            psD[:, eh * 512:(eh + 1) * 512],
                            hact[:, hb, sti * 128:(sti + 1) * 128],
                            wdT_b[:, hb, eh * 512:(eh + 1) * 512],
                            start=(hb == 0), stop=(hb == HC - 1)))
                dn_tmark.append(E.qgroup("t", fns, "t"))

                E.qwait("a", "t", dn_tmark[-1])
                g_amark.append(E.qsop("a", lambda a, q=q, sti=sti:
                                      a.activation(
                    out=g_b[:, q * 8 + sti, :], in_=psD[:],
                    func=AF.Silu, scale=SCALE * SCALE)))

    E.barrier()

    if nphase >= 6:
        kdma_m = {}
        vdma_m = {}
        kbuf_qk_last = {}
        vbuf_av_last = {}
        bank_user = {}
        obank_user = {}
        pt_last = {}
        exp_ms = {}
        xrs_m = {}
        cmb_last = [0]
        out_last = [0]
        pending = []
        gsub_c = [0]

        def issue_kbuf(s):
            if s >= NS:
                return
            ph4 = s % 4
            w = kbuf_qk_last.get(ph4)
            if w is not None:
                E.qwait("sy", "t", w)
            kb_t = kbuf if ph4 < 2 else kbufX
            E.qdma("sy", "kbuf%d" % ph4, lambda s_, sup=s, kb_t=kb_t,
                   ph4=ph4: s_.dma_start(
                       out=kb_t[:, ph4 % 2, :, :],
                       in_=kT_s[sup].rearrange("p (c q) -> p c q", q=512)))
            kdma_m[s] = E.mark("kbuf%d" % ph4)

        def issue_vbuf(s):
            if s >= NS:
                return
            ph2 = s % 2
            w = vbuf_av_last.get(ph2)
            if w is not None:
                E.qwait("pl", "t", w)
            E.qdma("pl", "vbuf%d" % ph2, lambda g, sup=s, ph2=ph2:
                   g.dma_start(
                       out=vbuf[:, ph2, :, :],
                       in_=vv_s[sup].rearrange("p (t e) -> p t e", e=DM)))
            vdma_m[s] = E.mark("vbuf%d" % ph2)

        def issue_xrs(k):
            if k >= NS:
                return
            E.qwait("pl", "v", cmb_last[0])
            E.qdma("pl", "xrs", lambda g, k=k: g.dma_start(
                out=xrs[:], in_=xq_h[k * 128:(k + 1) * 128, :]))
            xrs_m[k] = E.mark("xrs")

        def emit_finalize(s):
            E.qwait("v", "a")
            E.qsop("v", lambda v, s=s: v.reduce_sum(
                lsum[:, 0:1], lsb16[:, s, 0:s + 1], axis=AX.X))
            E.qsop("v", lambda v: v.reciprocal(linv[:, 0:1], lsum[:, 0:1]))
            E.qwait("v", "xrs", xrs_m[s])
            E.qwait("v", "out", out_last[0])
            E.qsop("v", lambda v, s=s: v.scalar_tensor_tensor(
                out=sg_t[:], in0=oacc[:, s, :], scalar=linv[:, 0:1],
                in1=g_b[:, s, :], op0=ALU.mult, op1=ALU.mult))
            cmb_last[0] = E.qsop("v", lambda v: v.tensor_add(
                outt[:], sg_t[:], xrs[:]))
            E.qwait("pl", "v", cmb_last[0])
            out_last[0] = E.qdma("pl", "out", lambda g, s=s: g.dma_start(
                out=out_h[s * 128:(s + 1) * 128, :], in_=outt[:]))
            issue_xrs(s + 1)

        def emit_qk(s, slots):
            gsub = gsub_c[0]
            gsub_c[0] += 1
            base = (gsub % 2) * 2
            pbr = gsub % 4
            ph4 = s % 4
            kb_t = kbuf if ph4 < 2 else kbufX
            E.qwait("t", "kbuf%d" % ph4, kdma_m[s])
            bw = bank_user.get(base)
            if bw is not None:
                E.qwait("t", "a", bw)
            if len(slots) == 2:
                bw = bank_user.get(base + 1)
                if bw is not None:
                    E.qwait("t", "a", bw)
            fns = []
            for pos, k in enumerate(slots):
                for e in range(DC):
                    fns.append(lambda t, e=e, k=k, bk=base + pos, kb_t=kb_t,
                               php=ph4 % 2: t.matmul(
                        psA[:, bk * 512:(bk + 1) * 512],
                        qT[:, e, k * 128:(k + 1) * 128],
                        kb_t[:, php, e, :],
                        start=(e == 0), stop=(e == DC - 1)))
            qk_m = E.qgroup("t", fns, "t")
            kbuf_qk_last[ph4] = qk_m

            msk = None
            if slots[0] == s:
                E.qwait("v", "t", qk_m)
                msk = E.qsop("v", lambda v, bk=base: v.tensor_tensor(
                    psA[:, bk * 512:(bk + 1) * 512],
                    psA[:, bk * 512:(bk + 1) * 512],
                    maskb[:], ALU.add))

            E.qwait("a", "t", qk_m)
            for pos, k in enumerate(slots):
                if pos == 0 and msk is not None:
                    E.qwait("a", "v", msk)
                exp_ms[(gsub, pos)] = E.qsop("a", lambda a, pos=pos, k=k,
                                             s=s, pbr=pbr, bk=base + pos:
                                             a.activation(
                    out=pbuf5[:, pbr, pos, :],
                    in_=psA[:, bk * 512:(bk + 1) * 512],
                    func=AF.Exp, scale=1.0 / 32.0,
                    accum_out=lsb16[:, k, s:s + 1]))
            if len(slots) == 2:
                bank_user[base + 1] = exp_ms[(gsub, 1)]
            pending.append((s, list(slots), gsub))

        def emit_trav(sub):
            s, slots, gsub = sub
            base = (gsub % 2) * 2
            pbr = gsub % 4
            ph2 = s % 2
            n = len(slots)
            for pos in range(n):
                E.qwait("t", "a", exp_ms[(gsub, pos)])
            tr_m = E.qgroup("t", [
                (lambda t, pos=pos, kt=kt, pbr=pbr, base=base: t.transpose(
                    trA[:, base * 1024 + pos * 512 + kt * 128:
                        base * 1024 + pos * 512 + (kt + 1) * 128],
                    pbuf5[:, pbr, pos, kt * 128:(kt + 1) * 128],
                    ident[:]))
                for pos in range(n) for kt in range(4)], "t")

            E.qwait("a", "t", tr_m)
            pw = pt_last.get(gsub % 2)
            if pw is not None:
                E.qwait("a", "t", pw)
            pte = E.qsop("a", lambda a, gp=gsub % 2, base=base, n=n:
                         a.activation(
                out=pT[:, gp, 0:n, :, :],
                in_=trA[:, base * 1024:base * 1024 + n * 512]
                .rearrange("p (s t q) -> p s t q", t=4, q=128),
                func=AF.Copy))
            bank_user[base] = pte

            E.qwait("t", "a", pte)
            E.qwait("t", "vbuf%d" % ph2, vdma_m[s])
            for pos, k in enumerate(slots):
                tgt = psBC if pos == 0 else psD
                ob = obank_user.get(pos)
                if ob is not None:
                    E.qwait("t", "v", ob)
                av = E.qgroup("t", [
                    (lambda t, kt=kt, eh=eh, pos=pos, gp=gsub % 2, ph2=ph2,
                     tgt=tgt: t.matmul(
                        tgt[:, eh * 512:(eh + 1) * 512],
                        pT[:, gp, pos, kt, :],
                        vbuf[:, ph2, kt, eh * 512:(eh + 1) * 512],
                        start=(kt == 0), stop=(kt == 3)))
                    for kt in range(4) for eh in range(2)], "t")
                vbuf_av_last[ph2] = av
                pt_last[gsub % 2] = av

                E.qwait("v", "t", av)
                if s == 0:
                    ev = E.qsop("v", lambda v, k=k, tgt=tgt:
                                v.tensor_scalar_add(
                        oacc[:, k, :], tgt[:], 0.0))
                else:
                    ev = E.qsop("v", lambda v, k=k, tgt=tgt: v.tensor_tensor(
                        oacc[:, k, :], tgt[:], oacc[:, k, :], ALU.add))
                obank_user[pos] = ev
            if slots[0] == s:
                emit_finalize(s)

        issue_kbuf(0)
        issue_kbuf(1)
        issue_kbuf(2)
        issue_vbuf(0)
        issue_vbuf(1)
        issue_xrs(0)
        for s in range(NS):
            issue_kbuf(s + 3)
            active = list(range(s, NS))
            for i in range(0, len(active), 2):
                emit_qk(s, active[i:i + 2])
                if len(pending) >= 2:
                    emit_trav(pending.pop(0))
                if i == 0 and s >= 1:
                    issue_vbuf(s + 1)
        while pending:
            emit_trav(pending.pop(0))

        E.qwait("pl", "out")

    with nc.Block() as blk:
        E.flush(blk)
    E.es.close()
    return nc


_NC_CACHE = {}


def _mk_inputs(X, wq, wk, wv, wg, wu, wd):
    ident = np.eye(128, dtype=np.float32).astype(BF)
    in_maps = []
    for core in range(8):
        b, j = core // 4, core % 4
        rows = np.arange(NS) * 4 + j
        xq = np.ascontiguousarray(
            X[b].reshape(64, 128, DM)[rows].reshape(QR, DM))
        c = np.arange(512)[None, :]
        r = np.arange(128)[:, None]
        mask = np.where(c <= 128 * j + r, 0.0, NEG).astype(np.float32)
        in_maps.append({
            "xk": X[b], "xq": xq,
            "wq": wq, "wk": wk, "wv": wv,
            "wg": wg, "wu": wu, "wd": wd,
            "mask": mask, "ident": ident,
        })
    return in_maps


def kernel(L0_attn_X, L0_attn_Wq, L0_attn_Wk, L0_attn_Wv,
           L0_ffn_w_gate, L0_ffn_w_up, L0_ffn_w_down):
    X = np.ascontiguousarray(np.asarray(L0_attn_X, dtype=np.float32))
    wq = np.ascontiguousarray(np.asarray(L0_attn_Wq, dtype=np.float32))
    wk = np.ascontiguousarray(np.asarray(L0_attn_Wk, dtype=np.float32))
    wv = np.ascontiguousarray(np.asarray(L0_attn_Wv, dtype=np.float32))
    wg = np.ascontiguousarray(np.asarray(L0_ffn_w_gate, dtype=np.int32))
    wu = np.ascontiguousarray(np.asarray(L0_ffn_w_up, dtype=np.int32))
    wd = np.ascontiguousarray(np.asarray(L0_ffn_w_down, dtype=np.int32))

    if "nc" not in _NC_CACHE:
        _NC_CACHE["nc"] = build()
    nc = _NC_CACHE["nc"]
    in_maps = _mk_inputs(X, wq, wk, wv, wg, wu, wd)
    res = run_bass_kernel_spmd(nc, in_maps, core_ids=list(range(8)))

    out = np.empty((2, S, DM), dtype=np.float32)
    for core in range(8):
        b, j = core // 4, core % 4
        rows = np.arange(NS) * 4 + j
        out[b].reshape(64, 128, DM)[rows] = \
            res.results[core]["out"].reshape(NS, 128, DM)
    return out


# version 29 (final)
# speedup vs baseline: 1.3218x; 1.3218x over previous
import contextlib
import numpy as np
import ml_dtypes
import concourse.bass as bass
import concourse.mybir as mybir
from concourse.bass_utils import run_bass_kernel_spmd

F32 = mybir.dt.float32
BF16 = mybir.dt.bfloat16
I32 = mybir.dt.int32
AF = mybir.ActivationFunctionType
ALU = mybir.AluOpType
AX = mybir.AxisListType
BF = ml_dtypes.bfloat16

S, DM, HID = 8192, 1024, 3072
NS = 16
QR = NS * 128
DC = DM // 128
HC = HID // 128
SCALE = 1.0 / 127.0
EPS = 1e-5
NEG = -30000.0
SELF_SYNC = set()
DBG_STATS = False
FAST_OK = set()


class Emit:

    def __init__(self, nc):
        self.nc = nc
        self.es = contextlib.ExitStack()
        self.sem = {}
        self.cnt = {}
        self.qs = {k: [] for k in ("sy", "pl", "v", "a", "t")}
        for name in ("v", "a", "t"):
            self._mk(name)

    def _mk(self, name):
        if name not in self.sem:
            self.sem[name] = self.es.enter_context(
                self.nc.semaphore("sem_" + name))
            self.cnt[name] = 0
        return self.sem[name]

    def qwait(self, ek, semkey, val=None):
        self._mk(semkey)
        v = self.cnt[semkey] if val is None else val
        self.qs[ek].append(("w", self.sem[semkey], v))

    def qop(self, ek, fn, inckey=None):
        mark = None
        if inckey is not None:
            self._mk(inckey)
            self.cnt[inckey] += 1
            mark = self.cnt[inckey]
            self.qs[ek].append(("o", fn, self.sem[inckey], 1))
        else:
            self.qs[ek].append(("o", fn, None, 0))
        return mark

    def qsop(self, ek, fn=None, fast=False, dep=None):
        key = {"v": "v", "a": "a", "t": "t"}[ek]
        if key == "v" and (fast is False or fast not in FAST_OK):
            if not (isinstance(dep, int) and dep < 0):
                self.qwait(ek, key, dep)
        return self.qop(ek, fn, key)

    def qdma(self, ek, semkey, fn):
        self._mk(semkey)
        self.cnt[semkey] += 16
        mark = self.cnt[semkey]
        self.qs[ek].append(("o", fn, self.sem[semkey], 16))
        return mark

    def qgroup(self, ek, fns, inckey):
        self._mk(inckey)
        self.cnt[inckey] += 1
        mark = self.cnt[inckey]
        self.qs[ek].append(("g", fns, self.sem[inckey], 1))
        return mark

    def mark(self, key):
        self._mk(key)
        return self.cnt[key]

    def barrier(self):
        marks = dict(self.cnt)
        for ek in self.qs:
            for kk, vv in marks.items():
                self.qs[ek].append(("w", self.sem[kk], vv))

    def flush(self, blk):
        def emit(eng, items):
            for it in items:
                if it[0] == "w":
                    eng.wait_ge(it[1], it[2])
                elif it[0] == "o":
                    ins = it[1](eng)
                    if it[2] is not None:
                        ins.then_inc(it[2], it[3])
                else:
                    for fn in it[1]:
                        ins = fn(eng)
                    ins.then_inc(it[2], it[3])

        @blk.sync
        def _(s):
            emit(s, self.qs["sy"])

        @blk.gpsimd
        def _(g):
            emit(g, self.qs["pl"])

        @blk.vector
        def _(v):
            emit(v, self.qs["v"])

        @blk.scalar
        def _(a):
            emit(a, self.qs["a"])

        @blk.tensor
        def _(t):
            emit(t, self.qs["t"])


def build(nphase=9):
    nc = bass.Bass("TRN2", target_bir_lowering=False, debug=False)

    xk_h = nc.dram_tensor("xk", [S, DM], F32, kind="ExternalInput").ap()
    xq_h = nc.dram_tensor("xq", [QR, DM], F32, kind="ExternalInput").ap()
    wq_h = nc.dram_tensor("wq", [DM, DM], F32, kind="ExternalInput").ap()
    wk_h = nc.dram_tensor("wk", [DM, DM], F32, kind="ExternalInput").ap()
    wv_h = nc.dram_tensor("wv", [DM, DM], F32, kind="ExternalInput").ap()
    wg_h = nc.dram_tensor("wg", [HID, DM], I32, kind="ExternalInput").ap()
    wu_h = nc.dram_tensor("wu", [HID, DM], I32, kind="ExternalInput").ap()
    wd_h = nc.dram_tensor("wd", [DM, HID], I32, kind="ExternalInput").ap()
    mask_h = nc.dram_tensor("mask", [128, 512], F32, kind="ExternalInput").ap()
    ident_h = nc.dram_tensor("ident", [128, 128], BF16, kind="ExternalInput").ap()
    out_h = nc.dram_tensor("out", [QR, DM], F32, kind="ExternalOutput").ap()

    kT_s = nc.dram_tensor("kT_s", [S // 512, 128, DC * 512], BF16,
                          kind="Internal").ap()
    vv_s = nc.dram_tensor("vv_s", [S // 512, 128, 4 * DM], BF16,
                          kind="Internal").ap()
    wgT_s = nc.dram_tensor("wgT_s", [DM, HID], BF16, kind="Internal").ap()
    wuT_s = nc.dram_tensor("wuT_s", [DM, HID], BF16, kind="Internal").ap()
    wdT_s = nc.dram_tensor("wdT_s", [HID, DM], BF16, kind="Internal").ap()

    BASE = 16896

    def at(name, shape, dtype, off):
        return nc.alloc_sbuf_tensor_at(name, shape, dtype, offset=BASE + off)

    wq_b = at("wq_b", [128, DC, DM], BF16, 0)
    wk_b = at("wk_b", [128, DC, DM], BF16, 16384)
    wv_b = at("wv_b", [128, DC, DM], BF16, 32768)
    xnq = at("xnq", [128, DC, QR], BF16, 49152)
    qT = at("qT", [128, DC, QR], BF16, 81920)
    g_b = at("g_b", [128, NS, DM], BF16, 114688)
    ident = at("ident_b", [128, 128], BF16, 147456)
    maskb = at("maskb", [128, 512], F32, 147712)
    bn6 = at("bn6", [128, 2, 12], F32, 149760)
    msb = at("msb", [128, 2, 16], F32, 149824)
    lsb = at("lsb", [128, 2, 16], F32, 149952)
    fsb = at("fsb", [128, 2, 16], F32, 150080)
    mglob = at("mglob", [128, 2], F32, 150208)
    lsum = at("lsum", [128, 2], F32, 150240)
    linv = at("linv", [128, 2], F32, 150272)
    veps = at("veps", [128, 80], F32, 150304)
    STG = 152064

    wst = at("wst", [128, HC, DM], BF16, 81920)
    wts = at("wts", [128, 4, 1536], BF16, 131072)
    xkb4 = at("xkb4", [128, 2, 4, DM], F32, STG)
    xnr = at("xnr", [128, 2, DM], BF16, STG + 32768)
    kst = at("kst", [128, DC, 512], BF16, STG + 36864)
    vst = at("vst", [128, 4, DM], BF16, STG + 45056)
    xns = at("xns", [128, 2, DC, 512], BF16, 49152)
    mvall = at("mvall", [128, 80, 2], F32, 150656)
    rstdall = at("rstdall", [128, 80], F32, 151296)
    wdT_b = at("wdT_b", [128, HC, DM], BF16, 0)
    hact = at("hact", [128, HC, DM], BF16, STG)
    wgu = at("wgu", [128, 2, 2, DC, 128], BF16, STG + 49152)
    silu_t = at("silu_t", [128, DM], BF16, STG + 57344)
    oacc = at("oacc", [128, NS, DM], F32, 0)
    pbuf5 = at("pbuf5", [128, 4, 2, 512], BF16, 65536)
    xrs = at("xrs", [128, DM], F32, 73728)
    sg_t = at("sg_t", [128, DM], F32, 77824)
    kbuf = at("kbuf", [128, 2, DC, 512], BF16, STG)
    kbufX = at("kbufX", [128, 2, DC, 512], BF16, STG + 16384)
    vbuf = at("vbuf", [128, 2, 4, DM], BF16, STG + 32768)
    pT = at("pT", [128, 2, 2, 4, 128], BF16, STG + 49152)
    outt = at("outt", [128, DM], F32, STG + 53248)
    lsb16 = at("lsb16", [128, NS, 16], F32, 150656)

    psA = nc.alloc_psum_tensor("psA", [128, 2048], F32)
    psBC = nc.alloc_psum_tensor("psBC", [128, 1024], F32)
    psD = nc.alloc_psum_tensor("psD", [128, 1024], F32)
    trA = psA.bitcast(BF16)
    trBC = psBC.bitcast(BF16)

    E = Emit(nc)

    for w_h, w_b in ((wk_h, wk_b), (wv_h, wv_b), (wq_h, wq_b)):
        E.qdma("pl", "wqkv", lambda g, w_h=w_h, w_b=w_b: g.dma_start(
            out=w_b[:], in_=w_h.rearrange("(c p) e -> p c e", p=128)))
    E.qdma("sy", "const", lambda s: s.dma_start(out=ident[:], in_=ident_h[:]))
    E.qdma("sy", "const", lambda s: s.dma_start(out=maskb[:], in_=mask_h[:]))
    sp_const = E.mark("const")

    rot = []
    wtasks = []
    wst_m = {}
    half_trm = {}
    wts_dma = []
    wts_final = {}
    wstf = wst.ap().rearrange("p a b -> p (a b)")
    halves = [(wi, hf) for wi in range(3) for hf in range(2)]

    def _half_params(h):
        wi, hf = halves[h]
        is_d = wi == 2
        nrow = DC if is_d else HC
        ncol = HC if is_d else DC
        wfree = HID if is_d else DM
        hr = nrow // 2
        return wi, hf, ncol, wfree, hr, hf * hr

    def issue_half_load(h):
        wi, hf, ncol, wfree, hr, r_lo = _half_params(h)
        w_h = (wg_h, wu_h, wd_h)[wi]
        hpar = h % 2
        qn = "pl"
        if h >= 2:
            E.qwait(qn, "t", half_trm[h - 2])
        E.qdma(qn, "wst%d" % hpar, lambda g, w_h=w_h, r_lo=r_lo, hr=hr,
               wfree=wfree, hpar=hpar: g.dma_start(
                   out=wstf[:, hpar * 12288:
                            hpar * 12288 + hr * wfree].rearrange(
                       "p (a b) -> p a b", b=wfree),
                   in_=w_h[r_lo * 128:(r_lo + hr) * 128, :].rearrange(
                       "(c p) e -> p c e", p=128)))
        wst_m[h] = E.mark("wst%d" % hpar)

    if nphase >= 1:
        issue_half_load(0)
        issue_half_load(1)
        for h in range(6):
            wi, hf, ncol, wfree, hr, r_lo = _half_params(h)
            nunit = (hr + 7) // 8
            for cb in range(ncol):
                for u in range(nunit):
                    r0, r1 = u * 8, min(hr, u * 8 + 8)
                    wtasks.append(("g", h, cb, r0, r1, wfree, r_lo, hr))
                wtasks.append(("d", h, cb, hr, r_lo))
            wtasks.append(("l", h))

    pump_on = [False]

    def pump(n):
        wT_tab = (wgT_s, wuT_s, wdT_s)
        while n > 0 and wtasks:
            task = wtasks.pop(0)
            if task[0] == "g":
                _, h, cb, r0, r1, wfree, r_lo, hr = task
                hpar = h % 2
                rb = len(rot) % 2
                E.qwait("t", "const", sp_const)
                E.qwait("t", "wst%d" % hpar, wst_m[h])
                if len(rot) >= 2:
                    E.qwait("t", "a", rot[-2])
                half_trm[h] = E.qgroup("t", [
                    (lambda t, r=r, r0=r0, cb=cb, rb=rb,
                     wfree=wfree, hpar=hpar: t.transpose(
                         trA[:, rb * 1024 + (r - r0) * 128:
                             rb * 1024 + (r - r0 + 1) * 128],
                         wstf[:, hpar * 12288 + r * wfree + cb * 128:
                              hpar * 12288 + r * wfree + (cb + 1) * 128],
                         ident[:]))
                    for r in range(r0, r1)], "t")
                E.qwait("a", "t")
                if len(wts_dma) >= 4:
                    E.qwait("a", *wts_dma[-4])
                rot.append(E.qsop("a", lambda a, r0=r0, r1=r1, rb=rb,
                                  wpar=cb % 4: a.activation(
                    out=wts[:, wpar, r0 * 128:r1 * 128],
                    in_=trA[:, rb * 1024:
                            rb * 1024 + (r1 - r0) * 128],
                    func=AF.Copy)))
                n -= 1
            elif task[0] == "d":
                _, h, cb, hr, r_lo = task
                wT_s = wT_tab[halves[h][0]]
                E.qwait("a", "a")
                wts_dma.append(("wTs%d" % (cb % 4), E.qdma(
                    "a", "wTs%d" % (cb % 4), lambda a, cb=cb, hr=hr,
                    r_lo=r_lo, wT_s=wT_s, wpar=cb % 4: a.dma_start(
                        out=wT_s[cb * 128:(cb + 1) * 128,
                                 r_lo * 128:(r_lo + hr) * 128],
                        in_=wts[:, wpar, 0:hr * 128]))))
            else:
                _, h = task
                if h + 2 < 6:
                    issue_half_load(h + 2)
        if not wtasks and not wts_final:
            for kk2 in ("wTs0", "wTs1", "wTs2", "wTs3"):
                wts_final[kk2] = E.mark(kk2)

    wgT_sv = wgT_s.rearrange("(c p) h -> p c h", p=128)
    wuT_sv = wuT_s.rearrange("(c p) h -> p c h", p=128)
    wgu_pref = {}

    def issue_wgu(hb):
        par = hb % 2
        E.qdma("sy", "wgu%d" % par, lambda s, hb=hb, par=par: s.dma_start(
            out=wgu[:, par, 0, :, :],
            in_=wgT_sv[:, :, hb * 128:(hb + 1) * 128]))
        E.qdma("sy", "wgu%d" % par, lambda s, hb=hb, par=par: s.dma_start(
            out=wgu[:, par, 1, :, :],
            in_=wuT_sv[:, :, hb * 128:(hb + 1) * 128]))
        return E.mark("wgu%d" % par)

    if nphase >= 2:
        nq_tiles = 16 if nphase >= 3 else 0

        aggr_m = []

        nmark = []
        entries = [(xk_h, b, True) for b in range(16)]
        if nq_tiles:
            entries += [(xq_h, b, False) for b in range(4)]
        NB = len(entries)

        tr_tm = []
        k_evac = []
        v_evac = []
        kv_tmark = []
        rstd_ms = {}
        ld_ms = {}

        def emit_load(gb):
            src_h, lb, kv = entries[gb]
            bpar = gb % 2
            if gb >= 2:
                E.qwait("sy", "v", nmark[-1])
            E.qdma("sy", "xkb4_%d" % bpar,
                   lambda s, lb=lb, bpar=bpar, src_h=src_h: s.dma_start(
                       out=xkb4[:, bpar, :, :],
                       in_=src_h[lb * 512:(lb + 1) * 512, :].rearrange(
                           "(ti p) d -> p ti d", p=128)))
            ld_ms[gb] = E.mark("xkb4_%d" % bpar)

        def emit_stats(gb):
            bpar = gb % 2
            g4 = 4 * gb
            E.qwait("v", "xkb4_%d" % bpar, ld_ms[gb])
            for ti in range(4):
                b2 = ti % 2
                E.qsop("v", lambda v, bpar=bpar, ti=ti, b2=b2: v.bn_stats(
                    bn6[:, b2, 0:6], xkb4[:, bpar, ti, 0:512]),
                    dep=(aggr_m[-2] if len(aggr_m) >= 2 else 0))
                E.qsop("v", lambda v, bpar=bpar, ti=ti, b2=b2: v.bn_stats(
                    bn6[:, b2, 6:12], xkb4[:, bpar, ti, 512:1024]),
                    dep=-10**9)
                aggr_m.append(E.qsop("v", lambda v, g4=g4, ti=ti, b2=b2:
                                     v.bn_aggr(
                    mvall[:, g4 + ti, :], bn6[:, b2, :])))
            E.qsop("v", lambda v, g4=g4: v.tensor_scalar_add(
                veps[:, g4:g4 + 4], mvall[:, g4:g4 + 4, 1:2], EPS))
            E.qsop("v", lambda v, g4=g4: v.tensor_scalar(
                out=rstdall[:, g4:g4 + 4], in0=veps[:, g4:g4 + 4],
                scalar1=0.5, scalar2=0.5, op0=ALU.mult, op1=ALU.add))
            E.qsop("v", lambda v, g4=g4: v.reciprocal(
                rstdall[:, g4:g4 + 4], rstdall[:, g4:g4 + 4]))
            for _ in range(5):
                E.qsop("v", lambda v, g4=g4: v.tensor_mul(
                    fsb[:, 0, 0:4], rstdall[:, g4:g4 + 4],
                    rstdall[:, g4:g4 + 4]))
                E.qsop("v", lambda v, g4=g4: v.tensor_mul(
                    fsb[:, 0, 0:4], fsb[:, 0, 0:4], veps[:, g4:g4 + 4]))
                E.qsop("v", lambda v: v.tensor_scalar(
                    out=fsb[:, 0, 0:4], in0=fsb[:, 0, 0:4],
                    scalar1=-0.5, scalar2=1.5, op0=ALU.mult, op1=ALU.add))
                E.qsop("v", lambda v, g4=g4: v.tensor_mul(
                    rstdall[:, g4:g4 + 4], rstdall[:, g4:g4 + 4],
                    fsb[:, 0, 0:4]))
            rstd_ms[gb] = E.mark("v")

        xnsm = {}

        def emit_ladder(gb):
            src_h, lb, kv = entries[gb]
            bpar = gb % 2
            pb = gb % 2
            g4 = 4 * gb
            for ti in range(4):
                tile = 4 * lb + ti
                par = ti % 2
                if len(tr_tm) >= 2:
                    E.qwait("v", "t", tr_tm[-2])
                v_m = E.qsop("v", dep=rstd_ms[gb], fn=lambda v,
                             bpar=bpar, ti=ti, par=par,
                             g4=g4: v.tensor_scalar(
                    out=xnr[:, par, :], in0=xkb4[:, bpar, ti, :],
                    scalar1=mvall[:, g4 + ti, 0:1],
                    scalar2=rstdall[:, g4 + ti:g4 + ti + 1],
                    op0=ALU.subtract, op1=ALU.mult))

                if ti == 3:
                    nmark.append(v_m)
                E.qwait("t", "v", v_m)
                rb = len(rot) % 2
                if len(rot) >= 2:
                    E.qwait("t", "a", rot[-2])
                tr_tm.append(E.qgroup("t", [
                    (lambda t, dcc=dcc, par=par, rb=rb: t.transpose(
                        trA[:, rb * 1024 + dcc * 128:
                            rb * 1024 + (dcc + 1) * 128],
                        xnr[:, par, dcc * 128:(dcc + 1) * 128],
                        ident[:]))
                    for dcc in range(DC)], "t"))

                E.qwait("a", "t")
                if kv:
                    fn = (lambda a, pb=pb, ti=ti, rb=rb: a.activation(
                        out=xns[:, pb, :, ti * 128:(ti + 1) * 128],
                        in_=trA[:, rb * 1024:(rb + 1) * 1024]
                        .rearrange("p (c q) -> p c q", q=128),
                        func=AF.Copy))
                else:
                    fn = (lambda a, tile=tile, rb=rb: a.activation(
                        out=xnq[:, :, tile * 128:(tile + 1) * 128],
                        in_=trA[:, rb * 1024:(rb + 1) * 1024]
                        .rearrange("p (c q) -> p c q", q=128),
                        func=AF.Copy))
                rot.append(E.qsop("a", fn))
            xnsm[gb] = E.mark("a")

        def emit_K(gb):
            src_h, lb, kv = entries[gb]
            pb = gb % 2
            a_xns = xnsm[gb]
            for e in range(DC):
                ph = e % 2
                if pump_on[0]:
                    pump(1)
                E.qwait("t", "a", a_xns)
                if len(k_evac) >= 2:
                    E.qwait("t", "a", k_evac[-2])
                kv_tmark.append(E.qgroup("t", [
                    (lambda t, dcc=dcc, pb=pb, e=e, ph=ph: t.matmul(
                        psBC[:, ph * 512:(ph + 1) * 512],
                        wk_b[:, dcc, e * 128:(e + 1) * 128],
                        xns[:, pb, dcc, :],
                        start=(dcc == 0), stop=(dcc == DC - 1)))
                    for dcc in range(DC)], "t"))

                E.qwait("a", "t", kv_tmark[-1])
                E.qwait("a", "kst")
                k_evac.append(E.qsop("a", lambda a, e=e, ph=ph:
                                     a.activation(
                    out=kst[:, e, :],
                    in_=psBC[:, ph * 512:(ph + 1) * 512],
                    func=AF.Copy)))
            E.qwait("pl", "a")
            E.qdma("pl", "kst", lambda g, lb=lb: g.dma_start(
                out=kT_s[lb],
                in_=kst.ap().rearrange("p c q -> p (c q)")))

        def emit_V(gb):
            src_h, lb, kv = entries[gb]
            pb = gb % 2
            for sti in range(4):
                if pump_on[0]:
                    pump(2)
                vtgt = psD[:, 0:1024] if sti % 2 == 0                     else psA[:, 1024:2048]
                if len(v_evac) >= 2:
                    E.qwait("t", "a", v_evac[-2])
                fns = []
                for dcc in range(DC):
                    for eh in range(2):
                        fns.append(lambda t, dcc=dcc, eh=eh, pb=pb,
                                   sti=sti, vtgt=vtgt: t.matmul(
                            vtgt[:, eh * 512:(eh + 1) * 512],
                            xns[:, pb, dcc,
                                sti * 128:(sti + 1) * 128],
                            wv_b[:, dcc, eh * 512:(eh + 1) * 512],
                            start=(dcc == 0), stop=(dcc == DC - 1)))
                kv_tmark.append(E.qgroup("t", fns, "t"))

                E.qwait("a", "t", kv_tmark[-1])
                E.qwait("a", "vst")
                v_evac.append(E.qsop("a", lambda a, sti=sti,
                                     vtgt=vtgt: a.activation(
                    out=vst[:, sti, :], in_=vtgt, func=AF.Copy)))

            E.qwait("pl", "a")
            E.qdma("pl", "vst", lambda g, lb=lb: g.dma_start(
                out=vv_s[lb],
                in_=vst.ap().rearrange("p t e -> p (t e)")))

        def emit_Q(sb):
            for e in range(DC):
                ph = e % 2
                if pump_on[0]:
                    pump(1)
                E.qwait("t", "a", xnsm[16 + sb])
                if len(k_evac) >= 2:
                    E.qwait("t", "a", k_evac[-2])
                kv_tmark.append(E.qgroup("t", [
                    (lambda t, dcc=dcc, sb=sb, e=e, ph=ph: t.matmul(
                        psBC[:, ph * 512:(ph + 1) * 512],
                        wq_b[:, dcc, e * 128:(e + 1) * 128],
                        xnq[:, dcc, sb * 512:(sb + 1) * 512],
                        start=(dcc == 0), stop=(dcc == DC - 1)))
                    for dcc in range(DC)], "t"))
                E.qwait("a", "t", kv_tmark[-1])
                k_evac.append(E.qsop("a", lambda a, sb=sb, e=e, ph=ph:
                                     a.activation(
                    out=qT[:, e, sb * 512:(sb + 1) * 512],
                    in_=psBC[:, ph * 512:(ph + 1) * 512],
                    func=AF.Copy)))

        E.qwait("t", "wqkv")
        emit_load(0)
        emit_stats(0)
        emit_load(1)
        emit_stats(1)
        emit_ladder(0)
        emit_load(2)
        for gb in range(NB):
            kv = entries[gb][2]
            boundary = kv and gb + 1 < NB and not entries[gb + 1][2]
            if gb == 2:
                pump_on[0] = True
            if kv:
                emit_K(gb)
            elif nphase >= 4:
                emit_Q(gb - 16)
            if not boundary:
                if gb + 1 < NB:
                    emit_ladder(gb + 1)
                if gb + 3 < NB:
                    emit_load(gb + 3)
                if gb + 2 < NB:
                    emit_stats(gb + 2)
            if kv:
                emit_V(gb)
            if boundary:
                emit_ladder(gb + 1)
                if gb + 3 < NB:
                    emit_load(gb + 3)
                if gb + 2 < NB:
                    emit_stats(gb + 2)
        pump(10**6)

        if nphase >= 5:
            for kk2 in ("wTs0", "wTs1", "wTs2", "wTs3"):
                E.qwait("sy", kk2, wts_final[kk2])
            wgu_pref[(0, 1)] = issue_wgu(1)
            E.qwait("sy", "vst")
            wgu_pref[(0, 0)] = issue_wgu(0)

    E.barrier()

    E.barrier()

    if nphase >= 5:
        E.qdma("sy", "wdT", lambda s: s.dma_start(
            out=wdT_b[:], in_=wdT_s.rearrange("(c p) e -> p c e", p=128)))
        sp_wd = E.mark("wdT")

        gu_tmark = []
        hm_vmark = []
        si_amark = []
        g_amark = []
        dn_tmark = []
        for q in range(2):
            qs = q * 1024
            for hb in range(HC):
                par = hb % 2
                if (q, hb) in wgu_pref:
                    sp_m = wgu_pref.pop((q, hb))
                else:
                    if len(gu_tmark) >= 2:
                        E.qwait("sy", "t", gu_tmark[-2][1])
                    sp_m = issue_wgu(hb)

                gb = (hb % 2) * 1024
                upD = hb % 2 == 1
                E.qwait("t", "wgu%d" % par, sp_m)
                if len(si_amark) >= 2:
                    E.qwait("t", "a", si_amark[-2])
                if len(hm_vmark) >= 2:
                    E.qwait("t", "v", hm_vmark[-2])
                if upD and g_amark:
                    E.qwait("t", "a", g_amark[-1])
                fns = []
                for dcc in range(DC):
                    for sh in range(2):
                        fns.append(lambda t, dcc=dcc, sh=sh, qs=qs, par=par,
                                   gb=gb: t.matmul(
                            psA[:, gb + sh * 512:gb + (sh + 1) * 512],
                            wgu[:, par, 0, dcc, :],
                            xnq[:, dcc, qs + sh * 512:qs + (sh + 1) * 512],
                            start=(dcc == 0), stop=(dcc == DC - 1)))
                m_g = E.qgroup("t", fns, "t")
                fns = []
                for dcc in range(DC):
                    for sh in range(2):
                        fns.append(lambda t, dcc=dcc, sh=sh, qs=qs, par=par,
                                   upD=upD: t.matmul(
                            (psD if upD else psBC)[:, sh * 512:
                                                   (sh + 1) * 512],
                            wgu[:, par, 1, dcc, :],
                            xnq[:, dcc, qs + sh * 512:qs + (sh + 1) * 512],
                            start=(dcc == 0), stop=(dcc == DC - 1)))
                m_u = E.qgroup("t", fns, "t")
                gu_tmark.append((m_g, m_u))

                E.qwait("a", "t", m_g)
                if hm_vmark:
                    E.qwait("a", "v", hm_vmark[-1])
                si_amark.append(E.qsop("a", lambda a, gb=gb: a.activation(
                    out=silu_t[:], in_=psA[:, gb:gb + 1024],
                    func=AF.Silu, scale=SCALE)))

                E.qwait("v", "t", m_u)
                E.qwait("v", "a", si_amark[-1])
                hm_vmark.append(E.qsop("v", lambda v, hb=hb, upD=upD:
                                       v.tensor_tensor(
                    hact[:, hb, :], silu_t[:], (psD if upD else psBC)[:],
                    ALU.mult)))
            v_hact = E.mark("v")

            for sti in range(8):
                E.qwait("t", "v", v_hact)
                E.qwait("t", "wdT", sp_wd)
                if len(g_amark) >= 1:
                    E.qwait("t", "a", g_amark[-1])
                fns = []
                for hb in range(HC):
                    for eh in range(2):
                        fns.append(lambda t, hb=hb, sti=sti, eh=eh: t.matmul(
                            psD[:, eh * 512:(eh + 1) * 512],
                            hact[:, hb, sti * 128:(sti + 1) * 128],
                            wdT_b[:, hb, eh * 512:(eh + 1) * 512],
                            start=(hb == 0), stop=(hb == HC - 1)))
                dn_tmark.append(E.qgroup("t", fns, "t"))

                E.qwait("a", "t", dn_tmark[-1])
                g_amark.append(E.qsop("a", lambda a, q=q, sti=sti:
                                      a.activation(
                    out=g_b[:, q * 8 + sti, :], in_=psD[:],
                    func=AF.Silu, scale=SCALE * SCALE)))

    E.barrier()

    if nphase >= 6:
        kdma_m = {}
        vdma_m = {}
        kbuf_qk_last = {}
        vbuf_av_last = {}
        bank_user = {}
        obank_user = {}
        pt_last = {}
        exp_ms = {}
        xrs_m = {}
        cmb_last = [0]
        out_last = [0]
        pending = []
        gsub_c = [0]

        def issue_kbuf(s):
            if s >= NS:
                return
            ph4 = s % 4
            w = kbuf_qk_last.get(ph4)
            if w is not None:
                E.qwait("sy", "t", w)
            kb_t = kbuf if ph4 < 2 else kbufX
            E.qdma("sy", "kbuf%d" % ph4, lambda s_, sup=s, kb_t=kb_t,
                   ph4=ph4: s_.dma_start(
                       out=kb_t[:, ph4 % 2, :, :],
                       in_=kT_s[sup].rearrange("p (c q) -> p c q", q=512)))
            kdma_m[s] = E.mark("kbuf%d" % ph4)

        def issue_vbuf(s):
            if s >= NS:
                return
            ph2 = s % 2
            w = vbuf_av_last.get(ph2)
            if w is not None:
                E.qwait("pl", "t", w)
            E.qdma("pl", "vbuf%d" % ph2, lambda g, sup=s, ph2=ph2:
                   g.dma_start(
                       out=vbuf[:, ph2, :, :],
                       in_=vv_s[sup].rearrange("p (t e) -> p t e", e=DM)))
            vdma_m[s] = E.mark("vbuf%d" % ph2)

        def issue_xrs(k):
            if k >= NS:
                return
            E.qwait("pl", "v", cmb_last[0])
            E.qdma("pl", "xrs", lambda g, k=k: g.dma_start(
                out=xrs[:], in_=xq_h[k * 128:(k + 1) * 128, :]))
            xrs_m[k] = E.mark("xrs")

        def emit_finalize(s):
            E.qwait("v", "a")
            E.qsop("v", lambda v, s=s: v.reduce_sum(
                lsum[:, 0:1], lsb16[:, s, 0:s + 1], axis=AX.X))
            E.qsop("v", lambda v: v.reciprocal(linv[:, 0:1], lsum[:, 0:1]))
            E.qwait("v", "xrs", xrs_m[s])
            E.qwait("v", "out", out_last[0])
            E.qsop("v", lambda v, s=s: v.scalar_tensor_tensor(
                out=sg_t[:], in0=oacc[:, s, :], scalar=linv[:, 0:1],
                in1=g_b[:, s, :], op0=ALU.mult, op1=ALU.mult))
            cmb_last[0] = E.qsop("v", lambda v: v.tensor_add(
                outt[:], sg_t[:], xrs[:]))
            E.qwait("pl", "v", cmb_last[0])
            out_last[0] = E.qdma("pl", "out", lambda g, s=s: g.dma_start(
                out=out_h[s * 128:(s + 1) * 128, :], in_=outt[:]))
            issue_xrs(s + 1)

        def emit_qk(s, slots):
            gsub = gsub_c[0]
            gsub_c[0] += 1
            base = (gsub % 2) * 2
            pbr = gsub % 4
            ph4 = s % 4
            kb_t = kbuf if ph4 < 2 else kbufX
            E.qwait("t", "kbuf%d" % ph4, kdma_m[s])
            bw = bank_user.get(base)
            if bw is not None:
                E.qwait("t", "a", bw)
            if len(slots) == 2:
                bw = bank_user.get(base + 1)
                if bw is not None:
                    E.qwait("t", "a", bw)
            fns = []
            for pos, k in enumerate(slots):
                for e in range(DC):
                    fns.append(lambda t, e=e, k=k, bk=base + pos, kb_t=kb_t,
                               php=ph4 % 2: t.matmul(
                        psA[:, bk * 512:(bk + 1) * 512],
                        qT[:, e, k * 128:(k + 1) * 128],
                        kb_t[:, php, e, :],
                        start=(e == 0), stop=(e == DC - 1)))
            qk_m = E.qgroup("t", fns, "t")
            kbuf_qk_last[ph4] = qk_m

            msk = None
            if slots[0] == s:
                E.qwait("v", "t", qk_m)
                msk = E.qsop("v", lambda v, bk=base: v.tensor_tensor(
                    psA[:, bk * 512:(bk + 1) * 512],
                    psA[:, bk * 512:(bk + 1) * 512],
                    maskb[:], ALU.add))

            E.qwait("a", "t", qk_m)
            for pos, k in enumerate(slots):
                if pos == 0 and msk is not None:
                    E.qwait("a", "v", msk)
                exp_ms[(gsub, pos)] = E.qsop("a", lambda a, pos=pos, k=k,
                                             s=s, pbr=pbr, bk=base + pos:
                                             a.activation(
                    out=pbuf5[:, pbr, pos, :],
                    in_=psA[:, bk * 512:(bk + 1) * 512],
                    func=AF.Exp, scale=1.0 / 32.0,
                    accum_out=lsb16[:, k, s:s + 1]))
            if len(slots) == 2:
                bank_user[base + 1] = exp_ms[(gsub, 1)]
            pending.append((s, list(slots), gsub))

        pte_ms = {}

        def emit_tr_pte(sub):
            s, slots, gsub = sub
            base = (gsub % 2) * 2
            pbr = gsub % 4
            n = len(slots)
            for pos in range(n):
                E.qwait("t", "a", exp_ms[(gsub, pos)])
            tr_m = E.qgroup("t", [
                (lambda t, pos=pos, kt=kt, pbr=pbr, base=base: t.transpose(
                    trA[:, base * 1024 + pos * 512 + kt * 128:
                        base * 1024 + pos * 512 + (kt + 1) * 128],
                    pbuf5[:, pbr, pos, kt * 128:(kt + 1) * 128],
                    ident[:]))
                for pos in range(n) for kt in range(4)], "t")

            E.qwait("a", "t", tr_m)
            pw = pt_last.get(gsub % 2)
            if pw is not None:
                E.qwait("a", "t", pw)
            pte = E.qsop("a", lambda a, gp=gsub % 2, base=base, n=n:
                         a.activation(
                out=pT[:, gp, 0:n, :, :],
                in_=trA[:, base * 1024:base * 1024 + n * 512]
                .rearrange("p (s t q) -> p s t q", t=4, q=128),
                func=AF.Copy))
            bank_user[base] = pte
            pte_ms[gsub] = pte

        def emit_av(sub):
            s, slots, gsub = sub
            ph2 = s % 2
            pte = pte_ms.pop(gsub)
            E.qwait("t", "a", pte)
            E.qwait("t", "vbuf%d" % ph2, vdma_m[s])
            for pos, k in enumerate(slots):
                tgt = psBC if pos == 0 else psD
                ob = obank_user.get(pos)
                if ob is not None:
                    E.qwait("t", "v", ob)
                av = E.qgroup("t", [
                    (lambda t, kt=kt, eh=eh, pos=pos, gp=gsub % 2, ph2=ph2,
                     tgt=tgt: t.matmul(
                        tgt[:, eh * 512:(eh + 1) * 512],
                        pT[:, gp, pos, kt, :],
                        vbuf[:, ph2, kt, eh * 512:(eh + 1) * 512],
                        start=(kt == 0), stop=(kt == 3)))
                    for kt in range(4) for eh in range(2)], "t")
                vbuf_av_last[ph2] = av
                pt_last[gsub % 2] = av

                E.qwait("v", "t", av)
                if s == 0:
                    ev = E.qsop("v", lambda v, k=k, tgt=tgt:
                                v.tensor_scalar_add(
                        oacc[:, k, :], tgt[:], 0.0))
                else:
                    ev = E.qsop("v", lambda v, k=k, tgt=tgt: v.tensor_tensor(
                        oacc[:, k, :], tgt[:], oacc[:, k, :], ALU.add))
                obank_user[pos] = ev
            if slots[0] == s:
                emit_finalize(s)

        issue_kbuf(0)
        issue_kbuf(1)
        issue_kbuf(2)
        issue_vbuf(0)
        issue_vbuf(1)
        issue_xrs(0)
        for s in range(NS):
            issue_kbuf(s + 3)
            active = list(range(s, NS))
            for i in range(0, len(active), 2):
                if pending:
                    emit_tr_pte(pending[0])
                emit_qk(s, active[i:i + 2])
                if len(pending) >= 2:
                    emit_av(pending.pop(0))
                if i == 0 and s >= 1:
                    issue_vbuf(s + 1)
        sub = pending.pop(0)
        emit_tr_pte(sub)
        emit_av(sub)

        E.qwait("pl", "out")

    with nc.Block() as blk:
        E.flush(blk)
    E.es.close()
    return nc


_NC_CACHE = {}


def _mk_inputs(X, wq, wk, wv, wg, wu, wd):
    ident = np.eye(128, dtype=np.float32).astype(BF)
    in_maps = []
    for core in range(8):
        b, j = core // 4, core % 4
        rows = np.arange(NS) * 4 + j
        xq = np.ascontiguousarray(
            X[b].reshape(64, 128, DM)[rows].reshape(QR, DM))
        c = np.arange(512)[None, :]
        r = np.arange(128)[:, None]
        mask = np.where(c <= 128 * j + r, 0.0, NEG).astype(np.float32)
        in_maps.append({
            "xk": X[b], "xq": xq,
            "wq": wq, "wk": wk, "wv": wv,
            "wg": wg, "wu": wu, "wd": wd,
            "mask": mask, "ident": ident,
        })
    return in_maps


def kernel(L0_attn_X, L0_attn_Wq, L0_attn_Wk, L0_attn_Wv,
           L0_ffn_w_gate, L0_ffn_w_up, L0_ffn_w_down):
    X = np.ascontiguousarray(np.asarray(L0_attn_X, dtype=np.float32))
    wq = np.ascontiguousarray(np.asarray(L0_attn_Wq, dtype=np.float32))
    wk = np.ascontiguousarray(np.asarray(L0_attn_Wk, dtype=np.float32))
    wv = np.ascontiguousarray(np.asarray(L0_attn_Wv, dtype=np.float32))
    wg = np.ascontiguousarray(np.asarray(L0_ffn_w_gate, dtype=np.int32))
    wu = np.ascontiguousarray(np.asarray(L0_ffn_w_up, dtype=np.int32))
    wd = np.ascontiguousarray(np.asarray(L0_ffn_w_down, dtype=np.int32))

    if "nc" not in _NC_CACHE:
        _NC_CACHE["nc"] = build()
    nc = _NC_CACHE["nc"]
    in_maps = _mk_inputs(X, wq, wk, wv, wg, wu, wd)
    res = run_bass_kernel_spmd(nc, in_maps, core_ids=list(range(8)))

    out = np.empty((2, S, DM), dtype=np.float32)
    for core in range(8):
        b, j = core // 4, core % 4
        rows = np.arange(NS) * 4 + j
        out[b].reshape(64, 128, DM)[rows] = \
            res.results[core]["out"].reshape(NS, 128, DM)
    return out


# version 37
# speedup vs baseline: 1.3481x; 1.0199x over previous
import contextlib
import numpy as np
import ml_dtypes
import concourse.bass as bass
import concourse.mybir as mybir
from concourse.bass_utils import run_bass_kernel_spmd

F32 = mybir.dt.float32
BF16 = mybir.dt.bfloat16
I32 = mybir.dt.int32
AF = mybir.ActivationFunctionType
ALU = mybir.AluOpType
AX = mybir.AxisListType
BF = ml_dtypes.bfloat16

S, DM, HID = 8192, 1024, 3072
NS = 16
QR = NS * 128
DC = DM // 128
HC = HID // 128
SCALE = 1.0 / 127.0
EPS = 1e-5
NEG = -30000.0
SELF_SYNC = set()
DBG_STATS = False
FAST_OK = set()


class Emit:

    def __init__(self, nc):
        self.nc = nc
        self.es = contextlib.ExitStack()
        self.sem = {}
        self.cnt = {}
        self.qs = {k: [] for k in ("sy", "pl", "v", "a", "t")}
        for name in ("v", "a", "t"):
            self._mk(name)

    def _mk(self, name):
        if name not in self.sem:
            self.sem[name] = self.es.enter_context(
                self.nc.semaphore("sem_" + name))
            self.cnt[name] = 0
        return self.sem[name]

    def qwait(self, ek, semkey, val=None):
        self._mk(semkey)
        v = self.cnt[semkey] if val is None else val
        self.qs[ek].append(("w", self.sem[semkey], v))

    def qop(self, ek, fn, inckey=None):
        mark = None
        if inckey is not None:
            self._mk(inckey)
            self.cnt[inckey] += 1
            mark = self.cnt[inckey]
            self.qs[ek].append(("o", fn, self.sem[inckey], 1))
        else:
            self.qs[ek].append(("o", fn, None, 0))
        return mark

    def qsop(self, ek, fn=None, fast=False, dep=None):
        key = {"v": "v", "a": "a", "t": "t"}[ek]
        if key == "v" and (fast is False or fast not in FAST_OK):
            if not (isinstance(dep, int) and dep < 0):
                self.qwait(ek, key, dep)
        return self.qop(ek, fn, key)

    def qdma(self, ek, semkey, fn):
        self._mk(semkey)
        self.cnt[semkey] += 16
        mark = self.cnt[semkey]
        self.qs[ek].append(("o", fn, self.sem[semkey], 16))
        return mark

    def qgroup(self, ek, fns, inckey):
        self._mk(inckey)
        self.cnt[inckey] += 1
        mark = self.cnt[inckey]
        self.qs[ek].append(("g", fns, self.sem[inckey], 1))
        return mark

    def mark(self, key):
        self._mk(key)
        return self.cnt[key]

    def barrier(self):
        marks = dict(self.cnt)
        for ek in self.qs:
            for kk, vv in marks.items():
                self.qs[ek].append(("w", self.sem[kk], vv))

    def flush(self, blk):
        def emit(eng, items):
            for it in items:
                if it[0] == "w":
                    eng.wait_ge(it[1], it[2])
                elif it[0] == "o":
                    ins = it[1](eng)
                    if it[2] is not None:
                        ins.then_inc(it[2], it[3])
                else:
                    for fn in it[1]:
                        ins = fn(eng)
                    ins.then_inc(it[2], it[3])

        @blk.sync
        def _(s):
            emit(s, self.qs["sy"])

        @blk.gpsimd
        def _(g):
            emit(g, self.qs["pl"])

        @blk.vector
        def _(v):
            emit(v, self.qs["v"])

        @blk.scalar
        def _(a):
            emit(a, self.qs["a"])

        @blk.tensor
        def _(t):
            emit(t, self.qs["t"])


def build(nphase=9):
    nc = bass.Bass("TRN2", target_bir_lowering=False, debug=False)

    xk_h = nc.dram_tensor("xk", [S, DM], F32, kind="ExternalInput").ap()
    xq_h = nc.dram_tensor("xq", [QR, DM], F32, kind="ExternalInput").ap()
    wq_h = nc.dram_tensor("wq", [DM, DM], F32, kind="ExternalInput").ap()
    wk_h = nc.dram_tensor("wk", [DM, DM], F32, kind="ExternalInput").ap()
    wv_h = nc.dram_tensor("wv", [DM, DM], F32, kind="ExternalInput").ap()
    wg_h = nc.dram_tensor("wg", [HID, DM], I32, kind="ExternalInput").ap()
    wu_h = nc.dram_tensor("wu", [HID, DM], I32, kind="ExternalInput").ap()
    wd_h = nc.dram_tensor("wd", [DM, HID], I32, kind="ExternalInput").ap()
    mask_h = nc.dram_tensor("mask", [128, 512], F32, kind="ExternalInput").ap()
    ident_h = nc.dram_tensor("ident", [128, 128], BF16, kind="ExternalInput").ap()
    out_h = nc.dram_tensor("out", [QR, DM], F32, kind="ExternalOutput").ap()

    kT_s = nc.dram_tensor("kT_s", [S // 512, 128, DC * 512], BF16,
                          kind="Internal").ap()
    vv_s = nc.dram_tensor("vv_s", [S // 512, 128, 4 * DM], BF16,
                          kind="Internal").ap()
    wgT_s = nc.dram_tensor("wgT_s", [DM, HID], BF16, kind="Internal").ap()
    wuT_s = nc.dram_tensor("wuT_s", [DM, HID], BF16, kind="Internal").ap()
    wdT_s = nc.dram_tensor("wdT_s", [HID, DM], BF16, kind="Internal").ap()

    BASE = 16896

    def at(name, shape, dtype, off):
        return nc.alloc_sbuf_tensor_at(name, shape, dtype, offset=BASE + off)

    wq_b = at("wq_b", [128, DC, DM], BF16, 0)
    wk_b = at("wk_b", [128, DC, DM], BF16, 16384)
    wv_b = at("wv_b", [128, DC, DM], BF16, 32768)
    xnq = at("xnq", [128, DC, QR], BF16, 49152)
    qT = at("qT", [128, DC, QR], BF16, 81920)
    g_b = at("g_b", [128, NS, DM], BF16, 114688)
    ident = at("ident_b", [128, 128], BF16, 147456)
    maskb = at("maskb", [128, 512], F32, 147712)
    bn6 = at("bn6", [128, 2, 12], F32, 149760)
    msb = at("msb", [128, 2, 16], F32, 149824)
    lsb = at("lsb", [128, 2, 16], F32, 149952)
    fsb = at("fsb", [128, 2, 16], F32, 150080)
    mglob = at("mglob", [128, 2], F32, 150208)
    lsum = at("lsum", [128, 2], F32, 150240)
    linv = at("linv", [128, 2], F32, 150272)
    veps = at("veps", [128, 80], F32, 150304)
    STG = 152064

    wst = at("wst", [128, HC, DM], BF16, 81920)
    wts = at("wts", [128, 4, 1536], BF16, 131072)
    xkb4 = at("xkb4", [128, 2, 4, DM], F32, STG)
    xnr = at("xnr", [128, 2, DM], BF16, STG + 32768)
    kst = at("kst", [128, DC, 512], BF16, STG + 36864)
    vst = at("vst", [128, 4, DM], BF16, STG + 45056)
    xns = at("xns", [128, 2, DC, 512], BF16, 49152)
    mvall = at("mvall", [128, 80, 2], F32, 150656)
    rstdall = at("rstdall", [128, 80], F32, 151296)
    wdT_b = at("wdT_b", [128, HC, DM], BF16, 0)
    hact = at("hact", [128, HC, DM], BF16, STG)
    wgu = at("wgu", [128, 2, 2, DC, 128], BF16, STG + 49152)
    silu_t = at("silu_t", [128, DM], BF16, STG + 57344)
    oacc = at("oacc", [128, NS, DM], F32, 0)
    pbuf5 = at("pbuf5", [128, 4, 2, 512], BF16, 65536)
    xrs = at("xrs", [128, DM], F32, 73728)
    sg_t = at("sg_t", [128, DM], F32, 77824)
    kbuf = at("kbuf", [128, 2, DC, 512], BF16, STG)
    kbufX = at("kbufX", [128, 2, DC, 512], BF16, STG + 16384)
    vbuf = at("vbuf", [128, 2, 4, DM], BF16, STG + 32768)
    pT = at("pT", [128, 2, 2, 4, 128], BF16, STG + 49152)
    outt = at("outt", [128, DM], F32, STG + 53248)
    lsb16 = at("lsb16", [128, NS, 16], F32, 150656)

    psA = nc.alloc_psum_tensor("psA", [128, 2048], F32)
    psBC = nc.alloc_psum_tensor("psBC", [128, 1024], F32)
    psD = nc.alloc_psum_tensor("psD", [128, 1024], F32)
    trA = psA.bitcast(BF16)
    trBC = psBC.bitcast(BF16)

    E = Emit(nc)

    E.qdma("sy", "const", lambda s: s.dma_start(out=ident[:], in_=ident_h[:]))
    E.qdma("sy", "const", lambda s: s.dma_start(out=maskb[:], in_=mask_h[:]))
    sp_const = E.mark("const")

    def load_wqkv():
        for w_h, w_b in ((wk_h, wk_b), (wv_h, wv_b), (wq_h, wq_b)):
            E.qdma("pl", "wqkv", lambda g, w_h=w_h, w_b=w_b: g.dma_start(
                out=w_b[:], in_=w_h.rearrange("(c p) e -> p c e", p=128)))

    rot = []
    wtasks = []
    wst_m = {}
    half_trm = {}
    wts_dma = []
    wts_final = {}
    wstf = wst.ap().rearrange("p a b -> p (a b)")
    halves = [(wi, hf) for wi in range(3) for hf in range(2)]

    def _half_params(h):
        wi, hf = halves[h]
        is_d = wi == 2
        nrow = DC if is_d else HC
        ncol = HC if is_d else DC
        wfree = HID if is_d else DM
        hr = nrow // 2
        return wi, hf, ncol, wfree, hr, hf * hr

    def issue_half_load(h):
        wi, hf, ncol, wfree, hr, r_lo = _half_params(h)
        w_h = (wg_h, wu_h, wd_h)[wi]
        hpar = h % 2
        qn = "pl"
        if h >= 2:
            E.qwait(qn, "t", half_trm[h - 2])
        E.qdma(qn, "wst%d" % hpar, lambda g, w_h=w_h, r_lo=r_lo, hr=hr,
               wfree=wfree, hpar=hpar: g.dma_start(
                   out=wstf[:, hpar * 12288:
                            hpar * 12288 + hr * wfree].rearrange(
                       "p (a b) -> p a b", b=wfree),
                   in_=w_h[r_lo * 128:(r_lo + hr) * 128, :].rearrange(
                       "(c p) e -> p c e", p=128)))
        wst_m[h] = E.mark("wst%d" % hpar)

    if nphase >= 1:
        issue_half_load(0)
        load_wqkv()
        issue_half_load(1)
    else:
        load_wqkv()
    if nphase >= 1:
        for h in range(6):
            wi, hf, ncol, wfree, hr, r_lo = _half_params(h)
            nunit = (hr + 7) // 8
            for cb in range(ncol):
                for u in range(nunit):
                    r0, r1 = u * 8, min(hr, u * 8 + 8)
                    wtasks.append(("g", h, cb, r0, r1, wfree, r_lo, hr))
                wtasks.append(("d", h, cb, hr, r_lo))
            wtasks.append(("l", h))

    pump_on = [False]

    def pump(n):
        wT_tab = (wgT_s, wuT_s, wdT_s)
        while n > 0 and wtasks:
            task = wtasks.pop(0)
            if task[0] == "g":
                _, h, cb, r0, r1, wfree, r_lo, hr = task
                hpar = h % 2
                rb = len(rot) % 2
                E.qwait("t", "const", sp_const)
                E.qwait("t", "wst%d" % hpar, wst_m[h])
                if len(rot) >= 2:
                    E.qwait("t", "a", rot[-2])
                half_trm[h] = E.qgroup("t", [
                    (lambda t, r=r, r0=r0, cb=cb, rb=rb,
                     wfree=wfree, hpar=hpar: t.transpose(
                         trA[:, rb * 1024 + (r - r0) * 128:
                             rb * 1024 + (r - r0 + 1) * 128],
                         wstf[:, hpar * 12288 + r * wfree + cb * 128:
                              hpar * 12288 + r * wfree + (cb + 1) * 128],
                         ident[:]))
                    for r in range(r0, r1)], "t")
                E.qwait("a", "t")
                if len(wts_dma) >= 4:
                    E.qwait("a", *wts_dma[-4])
                rot.append(E.qsop("a", lambda a, r0=r0, r1=r1, rb=rb,
                                  wpar=cb % 4: a.activation(
                    out=wts[:, wpar, r0 * 128:r1 * 128],
                    in_=trA[:, rb * 1024:
                            rb * 1024 + (r1 - r0) * 128],
                    func=AF.Copy)))
                n -= 1
            elif task[0] == "d":
                _, h, cb, hr, r_lo = task
                wT_s = wT_tab[halves[h][0]]
                E.qwait("a", "a")
                wts_dma.append(("wTs%d" % (cb % 4), E.qdma(
                    "a", "wTs%d" % (cb % 4), lambda a, cb=cb, hr=hr,
                    r_lo=r_lo, wT_s=wT_s, wpar=cb % 4: a.dma_start(
                        out=wT_s[cb * 128:(cb + 1) * 128,
                                 r_lo * 128:(r_lo + hr) * 128],
                        in_=wts[:, wpar, 0:hr * 128]))))
            else:
                _, h = task
                if h + 2 < 6:
                    issue_half_load(h + 2)
        if not wtasks and not wts_final:
            for kk2 in ("wTs0", "wTs1", "wTs2", "wTs3"):
                wts_final[kk2] = E.mark(kk2)

    wgT_sv = wgT_s.rearrange("(c p) h -> p c h", p=128)
    wuT_sv = wuT_s.rearrange("(c p) h -> p c h", p=128)
    wgu_pref = {}

    def issue_wgu(hb):
        par = hb % 2
        E.qdma("sy", "wgu%d" % par, lambda s, hb=hb, par=par: s.dma_start(
            out=wgu[:, par, 0, :, :],
            in_=wgT_sv[:, :, hb * 128:(hb + 1) * 128]))
        E.qdma("sy", "wgu%d" % par, lambda s, hb=hb, par=par: s.dma_start(
            out=wgu[:, par, 1, :, :],
            in_=wuT_sv[:, :, hb * 128:(hb + 1) * 128]))
        return E.mark("wgu%d" % par)

    if nphase >= 2:
        nq_tiles = 16 if nphase >= 3 else 0

        aggr_m = []

        nmark = []
        entries = [(xk_h, b, True) for b in range(16)]
        if nq_tiles:
            entries += [(xq_h, b, False) for b in range(4)]
        NB = len(entries)

        tr_tm = []
        k_evac = []
        v_evac = []
        kv_tmark = []
        rstd_ms = {}
        ld_ms = {}

        def emit_load(gb):
            src_h, lb, kv = entries[gb]
            bpar = gb % 2
            if gb >= 2:
                E.qwait("sy", "v", nmark[-1])
            E.qdma("sy", "xkb4_%d" % bpar,
                   lambda s, lb=lb, bpar=bpar, src_h=src_h: s.dma_start(
                       out=xkb4[:, bpar, :, :],
                       in_=src_h[lb * 512:(lb + 1) * 512, :].rearrange(
                           "(ti p) d -> p ti d", p=128)))
            ld_ms[gb] = E.mark("xkb4_%d" % bpar)

        def emit_stats(gb):
            bpar = gb % 2
            g4 = 4 * gb
            E.qwait("v", "xkb4_%d" % bpar, ld_ms[gb])
            for ti in range(4):
                b2 = ti % 2
                E.qsop("v", lambda v, bpar=bpar, ti=ti, b2=b2: v.bn_stats(
                    bn6[:, b2, 0:6], xkb4[:, bpar, ti, 0:512]),
                    dep=(aggr_m[-2] if len(aggr_m) >= 2 else 0))
                E.qsop("v", lambda v, bpar=bpar, ti=ti, b2=b2: v.bn_stats(
                    bn6[:, b2, 6:12], xkb4[:, bpar, ti, 512:1024]),
                    dep=-10**9)
                aggr_m.append(E.qsop("v", lambda v, g4=g4, ti=ti, b2=b2:
                                     v.bn_aggr(
                    mvall[:, g4 + ti, :], bn6[:, b2, :])))
            E.qsop("v", lambda v, g4=g4: v.tensor_scalar_add(
                veps[:, g4:g4 + 4], mvall[:, g4:g4 + 4, 1:2], EPS))
            E.qsop("v", lambda v, g4=g4: v.tensor_scalar(
                out=rstdall[:, g4:g4 + 4], in0=veps[:, g4:g4 + 4],
                scalar1=0.5, scalar2=0.5, op0=ALU.mult, op1=ALU.add))
            E.qsop("v", lambda v, g4=g4: v.reciprocal(
                rstdall[:, g4:g4 + 4], rstdall[:, g4:g4 + 4]))
            for _ in range(5):
                E.qsop("v", lambda v, g4=g4: v.tensor_mul(
                    fsb[:, 0, 0:4], rstdall[:, g4:g4 + 4],
                    rstdall[:, g4:g4 + 4]))
                E.qsop("v", lambda v, g4=g4: v.tensor_mul(
                    fsb[:, 0, 0:4], fsb[:, 0, 0:4], veps[:, g4:g4 + 4]))
                E.qsop("v", lambda v: v.tensor_scalar(
                    out=fsb[:, 0, 0:4], in0=fsb[:, 0, 0:4],
                    scalar1=-0.5, scalar2=1.5, op0=ALU.mult, op1=ALU.add))
                E.qsop("v", lambda v, g4=g4: v.tensor_mul(
                    rstdall[:, g4:g4 + 4], rstdall[:, g4:g4 + 4],
                    fsb[:, 0, 0:4]))
            rstd_ms[gb] = E.mark("v")

        xnsm = {}

        def emit_ladder(gb):
            src_h, lb, kv = entries[gb]
            bpar = gb % 2
            pb = gb % 2
            g4 = 4 * gb
            for ti in range(4):
                tile = 4 * lb + ti
                par = ti % 2
                if len(tr_tm) >= 2:
                    E.qwait("v", "t", tr_tm[-2])
                v_m = E.qsop("v", dep=rstd_ms[gb], fn=lambda v,
                             bpar=bpar, ti=ti, par=par,
                             g4=g4: v.tensor_scalar(
                    out=xnr[:, par, :], in0=xkb4[:, bpar, ti, :],
                    scalar1=mvall[:, g4 + ti, 0:1],
                    scalar2=rstdall[:, g4 + ti:g4 + ti + 1],
                    op0=ALU.subtract, op1=ALU.mult))

                if ti == 3:
                    nmark.append(v_m)
                E.qwait("t", "v", v_m)
                rb = len(rot) % 2
                if len(rot) >= 2:
                    E.qwait("t", "a", rot[-2])
                tr_tm.append(E.qgroup("t", [
                    (lambda t, dcc=dcc, par=par, rb=rb: t.transpose(
                        trA[:, rb * 1024 + dcc * 128:
                            rb * 1024 + (dcc + 1) * 128],
                        xnr[:, par, dcc * 128:(dcc + 1) * 128],
                        ident[:]))
                    for dcc in range(DC)], "t"))

                E.qwait("a", "t")
                if kv:
                    fn = (lambda a, pb=pb, ti=ti, rb=rb: a.activation(
                        out=xns[:, pb, :, ti * 128:(ti + 1) * 128],
                        in_=trA[:, rb * 1024:(rb + 1) * 1024]
                        .rearrange("p (c q) -> p c q", q=128),
                        func=AF.Copy))
                else:
                    fn = (lambda a, tile=tile, rb=rb: a.activation(
                        out=xnq[:, :, tile * 128:(tile + 1) * 128],
                        in_=trA[:, rb * 1024:(rb + 1) * 1024]
                        .rearrange("p (c q) -> p c q", q=128),
                        func=AF.Copy))
                rot.append(E.qsop("a", fn))
            xnsm[gb] = E.mark("a")

        def emit_K(gb):
            src_h, lb, kv = entries[gb]
            pb = gb % 2
            a_xns = xnsm[gb]
            for e in range(DC):
                ph = e % 2
                if pump_on[0]:
                    pump(1)
                E.qwait("t", "a", a_xns)
                if len(k_evac) >= 2:
                    E.qwait("t", "a", k_evac[-2])
                kv_tmark.append(E.qgroup("t", [
                    (lambda t, dcc=dcc, pb=pb, e=e, ph=ph: t.matmul(
                        psBC[:, ph * 512:(ph + 1) * 512],
                        wk_b[:, dcc, e * 128:(e + 1) * 128],
                        xns[:, pb, dcc, :],
                        start=(dcc == 0), stop=(dcc == DC - 1)))
                    for dcc in range(DC)], "t"))

                E.qwait("a", "t", kv_tmark[-1])
                E.qwait("a", "kst")
                k_evac.append(E.qsop("a", lambda a, e=e, ph=ph:
                                     a.activation(
                    out=kst[:, e, :],
                    in_=psBC[:, ph * 512:(ph + 1) * 512],
                    func=AF.Copy)))
            E.qwait("pl", "a")
            E.qdma("pl", "kst", lambda g, lb=lb: g.dma_start(
                out=kT_s[lb],
                in_=kst.ap().rearrange("p c q -> p (c q)")))

        def emit_V(gb):
            src_h, lb, kv = entries[gb]
            pb = gb % 2
            for sti in range(4):
                if pump_on[0]:
                    pump(2)
                vtgt = psD[:, 0:1024] if sti % 2 == 0                     else psA[:, 1024:2048]
                if len(v_evac) >= 2:
                    E.qwait("t", "a", v_evac[-2])
                fns = []
                for dcc in range(DC):
                    for eh in range(2):
                        fns.append(lambda t, dcc=dcc, eh=eh, pb=pb,
                                   sti=sti, vtgt=vtgt: t.matmul(
                            vtgt[:, eh * 512:(eh + 1) * 512],
                            xns[:, pb, dcc,
                                sti * 128:(sti + 1) * 128],
                            wv_b[:, dcc, eh * 512:(eh + 1) * 512],
                            start=(dcc == 0), stop=(dcc == DC - 1)))
                kv_tmark.append(E.qgroup("t", fns, "t"))

                E.qwait("a", "t", kv_tmark[-1])
                E.qwait("a", "vst")
                v_evac.append(E.qsop("a", lambda a, sti=sti,
                                     vtgt=vtgt: a.activation(
                    out=vst[:, sti, :], in_=vtgt, func=AF.Copy)))

            E.qwait("pl", "a")
            E.qdma("pl", "vst", lambda g, lb=lb: g.dma_start(
                out=vv_s[lb],
                in_=vst.ap().rearrange("p t e -> p (t e)")))

        def emit_Q(sb):
            for e in range(DC):
                ph = e % 2
                if pump_on[0]:
                    pump(1)
                E.qwait("t", "a", xnsm[16 + sb])
                if len(k_evac) >= 2:
                    E.qwait("t", "a", k_evac[-2])
                kv_tmark.append(E.qgroup("t", [
                    (lambda t, dcc=dcc, sb=sb, e=e, ph=ph: t.matmul(
                        psBC[:, ph * 512:(ph + 1) * 512],
                        wq_b[:, dcc, e * 128:(e + 1) * 128],
                        xnq[:, dcc, sb * 512:(sb + 1) * 512],
                        start=(dcc == 0), stop=(dcc == DC - 1)))
                    for dcc in range(DC)], "t"))
                E.qwait("a", "t", kv_tmark[-1])
                k_evac.append(E.qsop("a", lambda a, sb=sb, e=e, ph=ph:
                                     a.activation(
                    out=qT[:, e, sb * 512:(sb + 1) * 512],
                    in_=psBC[:, ph * 512:(ph + 1) * 512],
                    func=AF.Copy)))

        pump(12)
        E.qwait("t", "wqkv")
        emit_load(0)
        emit_stats(0)
        emit_load(1)
        emit_stats(1)
        emit_ladder(0)
        emit_load(2)
        for gb in range(NB):
            kv = entries[gb][2]
            boundary = kv and gb + 1 < NB and not entries[gb + 1][2]
            if gb == 2:
                pump_on[0] = True
            if kv:
                emit_K(gb)
            elif nphase >= 4:
                emit_Q(gb - 16)
            if not boundary:
                if gb + 1 < NB:
                    emit_ladder(gb + 1)
                if gb + 3 < NB:
                    emit_load(gb + 3)
                if gb + 2 < NB:
                    emit_stats(gb + 2)
            if kv:
                emit_V(gb)
            if boundary:
                emit_ladder(gb + 1)
                if gb + 3 < NB:
                    emit_load(gb + 3)
                if gb + 2 < NB:
                    emit_stats(gb + 2)
        pump(10**6)

        if nphase >= 5:
            for kk2 in ("wTs0", "wTs1", "wTs2", "wTs3"):
                E.qwait("sy", kk2, wts_final[kk2])
            wgu_pref[(0, 1)] = issue_wgu(1)
            E.qwait("sy", "vst")
            wgu_pref[(0, 0)] = issue_wgu(0)

    E.barrier()

    E.barrier()

    if nphase >= 5:
        E.qdma("sy", "wdT", lambda s: s.dma_start(
            out=wdT_b[:], in_=wdT_s.rearrange("(c p) e -> p c e", p=128)))
        sp_wd = E.mark("wdT")

        gu_tmark = []
        hm_vmark = []
        si_amark = []
        g_amark = []
        dn_tmark = []
        for q in range(2):
            qs = q * 1024
            for hb in range(HC):
                par = hb % 2
                if (q, hb) in wgu_pref:
                    sp_m = wgu_pref.pop((q, hb))
                else:
                    if len(gu_tmark) >= 2:
                        E.qwait("sy", "t", gu_tmark[-2][1])
                    sp_m = issue_wgu(hb)

                gb = (hb % 2) * 1024
                upD = hb % 2 == 1
                E.qwait("t", "wgu%d" % par, sp_m)
                if len(si_amark) >= 2:
                    E.qwait("t", "a", si_amark[-2])
                if len(hm_vmark) >= 2:
                    E.qwait("t", "v", hm_vmark[-2])
                if g_amark:
                    E.qwait("t", "a", g_amark[-1])
                fns = []
                for dcc in range(DC):
                    for sh in range(2):
                        fns.append(lambda t, dcc=dcc, sh=sh, qs=qs, par=par,
                                   gb=gb: t.matmul(
                            psA[:, gb + sh * 512:gb + (sh + 1) * 512],
                            wgu[:, par, 0, dcc, :],
                            xnq[:, dcc, qs + sh * 512:qs + (sh + 1) * 512],
                            start=(dcc == 0), stop=(dcc == DC - 1)))
                m_g = E.qgroup("t", fns, "t")
                fns = []
                for dcc in range(DC):
                    for sh in range(2):
                        fns.append(lambda t, dcc=dcc, sh=sh, qs=qs, par=par,
                                   upD=upD: t.matmul(
                            (psD if upD else psBC)[:, sh * 512:
                                                   (sh + 1) * 512],
                            wgu[:, par, 1, dcc, :],
                            xnq[:, dcc, qs + sh * 512:qs + (sh + 1) * 512],
                            start=(dcc == 0), stop=(dcc == DC - 1)))
                m_u = E.qgroup("t", fns, "t")
                gu_tmark.append((m_g, m_u))

                E.qwait("a", "t", m_g)
                if hm_vmark:
                    E.qwait("a", "v", hm_vmark[-1])
                si_amark.append(E.qsop("a", lambda a, gb=gb: a.activation(
                    out=silu_t[:], in_=psA[:, gb:gb + 1024],
                    func=AF.Silu, scale=SCALE)))

                E.qwait("v", "t", m_u)
                E.qwait("v", "a", si_amark[-1])
                hm_vmark.append(E.qsop("v", lambda v, hb=hb, upD=upD:
                                       v.tensor_tensor(
                    hact[:, hb, :], silu_t[:], (psD if upD else psBC)[:],
                    ALU.mult)))
            v_hact = E.mark("v")

            for sti in range(8):
                dnps = psD if sti % 2 == 0 else psBC
                E.qwait("t", "v", v_hact)
                E.qwait("t", "wdT", sp_wd)
                if len(g_amark) >= 2:
                    E.qwait("t", "a", g_amark[-2])
                fns = []
                for hb in range(HC):
                    for eh in range(2):
                        fns.append(lambda t, hb=hb, sti=sti, eh=eh,
                                   dnps=dnps: t.matmul(
                            dnps[:, eh * 512:(eh + 1) * 512],
                            hact[:, hb, sti * 128:(sti + 1) * 128],
                            wdT_b[:, hb, eh * 512:(eh + 1) * 512],
                            start=(hb == 0), stop=(hb == HC - 1)))
                dn_tmark.append(E.qgroup("t", fns, "t"))

                E.qwait("a", "t", dn_tmark[-1])
                g_amark.append(E.qsop("a", lambda a, q=q, sti=sti,
                                      dnps=dnps: a.activation(
                    out=g_b[:, q * 8 + sti, :], in_=dnps[:],
                    func=AF.Silu, scale=SCALE * SCALE)))

    E.barrier()

    if nphase >= 6:
        kdma_m = {}
        vdma_m = {}
        kbuf_qk_last = {}
        vbuf_av_last = {}
        bank_user = {}
        obank_user = {}
        pt_last = {}
        exp_ms = {}
        xrs_m = {}
        cmb_last = [0]
        out_last = [0]
        pending = []
        gsub_c = [0]

        def issue_kbuf(s):
            if s >= NS:
                return
            ph4 = s % 4
            w = kbuf_qk_last.get(ph4)
            if w is not None:
                E.qwait("sy", "t", w)
            kb_t = kbuf if ph4 < 2 else kbufX
            E.qdma("sy", "kbuf%d" % ph4, lambda s_, sup=s, kb_t=kb_t,
                   ph4=ph4: s_.dma_start(
                       out=kb_t[:, ph4 % 2, :, :],
                       in_=kT_s[sup].rearrange("p (c q) -> p c q", q=512)))
            kdma_m[s] = E.mark("kbuf%d" % ph4)

        def issue_vbuf(s):
            if s >= NS:
                return
            ph2 = s % 2
            w = vbuf_av_last.get(ph2)
            if w is not None:
                E.qwait("pl", "t", w)
            E.qdma("pl", "vbuf%d" % ph2, lambda g, sup=s, ph2=ph2:
                   g.dma_start(
                       out=vbuf[:, ph2, :, :],
                       in_=vv_s[sup].rearrange("p (t e) -> p t e", e=DM)))
            vdma_m[s] = E.mark("vbuf%d" % ph2)

        def issue_xrs(k):
            if k >= NS:
                return
            E.qwait("pl", "v", cmb_last[0])
            E.qdma("pl", "xrs", lambda g, k=k: g.dma_start(
                out=xrs[:], in_=xq_h[k * 128:(k + 1) * 128, :]))
            xrs_m[k] = E.mark("xrs")

        def emit_finalize(s):
            E.qwait("v", "a")
            E.qsop("v", lambda v, s=s: v.reduce_sum(
                lsum[:, 0:1], lsb16[:, s, 0:s + 1], axis=AX.X))
            E.qsop("v", lambda v: v.reciprocal(linv[:, 0:1], lsum[:, 0:1]))
            E.qwait("v", "xrs", xrs_m[s])
            E.qwait("v", "out", out_last[0])
            E.qsop("v", lambda v, s=s: v.scalar_tensor_tensor(
                out=sg_t[:], in0=oacc[:, s, :], scalar=linv[:, 0:1],
                in1=g_b[:, s, :], op0=ALU.mult, op1=ALU.mult))
            cmb_last[0] = E.qsop("v", lambda v: v.tensor_add(
                outt[:], sg_t[:], xrs[:]))
            E.qwait("pl", "v", cmb_last[0])
            out_last[0] = E.qdma("pl", "out", lambda g, s=s: g.dma_start(
                out=out_h[s * 128:(s + 1) * 128, :], in_=outt[:]))
            issue_xrs(s + 1)

        def emit_qk(s, slots):
            gsub = gsub_c[0]
            gsub_c[0] += 1
            base = (gsub % 2) * 2
            pbr = gsub % 4
            ph4 = s % 4
            kb_t = kbuf if ph4 < 2 else kbufX
            E.qwait("t", "kbuf%d" % ph4, kdma_m[s])
            bw = bank_user.get(base)
            if bw is not None:
                E.qwait("t", "a", bw)
            if len(slots) == 2:
                bw = bank_user.get(base + 1)
                if bw is not None:
                    E.qwait("t", "a", bw)
            fns = []
            for pos, k in enumerate(slots):
                for e in range(DC):
                    fns.append(lambda t, e=e, k=k, bk=base + pos, kb_t=kb_t,
                               php=ph4 % 2: t.matmul(
                        psA[:, bk * 512:(bk + 1) * 512],
                        qT[:, e, k * 128:(k + 1) * 128],
                        kb_t[:, php, e, :],
                        start=(e == 0), stop=(e == DC - 1)))
            qk_m = E.qgroup("t", fns, "t")
            kbuf_qk_last[ph4] = qk_m

            msk = None
            if slots[0] == s:
                E.qwait("v", "t", qk_m)
                msk = E.qsop("v", lambda v, bk=base: v.tensor_tensor(
                    psA[:, bk * 512:(bk + 1) * 512],
                    psA[:, bk * 512:(bk + 1) * 512],
                    maskb[:], ALU.add))

            E.qwait("a", "t", qk_m)
            for pos, k in enumerate(slots):
                if pos == 0 and msk is not None:
                    E.qwait("a", "v", msk)
                exp_ms[(gsub, pos)] = E.qsop("a", lambda a, pos=pos, k=k,
                                             s=s, pbr=pbr, bk=base + pos:
                                             a.activation(
                    out=pbuf5[:, pbr, pos, :],
                    in_=psA[:, bk * 512:(bk + 1) * 512],
                    func=AF.Exp, scale=1.0 / 32.0,
                    accum_out=lsb16[:, k, s:s + 1]))
            if len(slots) == 2:
                bank_user[base + 1] = exp_ms[(gsub, 1)]
            pending.append((s, list(slots), gsub))

        pte_ms = {}

        def emit_tr_pte(sub):
            s, slots, gsub = sub
            base = (gsub % 2) * 2
            pbr = gsub % 4
            n = len(slots)
            for pos in range(n):
                E.qwait("t", "a", exp_ms[(gsub, pos)])
            tr_m = E.qgroup("t", [
                (lambda t, pos=pos, kt=kt, pbr=pbr, base=base: t.transpose(
                    trA[:, base * 1024 + pos * 512 + kt * 128:
                        base * 1024 + pos * 512 + (kt + 1) * 128],
                    pbuf5[:, pbr, pos, kt * 128:(kt + 1) * 128],
                    ident[:]))
                for pos in range(n) for kt in range(4)], "t")

            E.qwait("a", "t", tr_m)
            pw = pt_last.get(gsub % 2)
            if pw is not None:
                E.qwait("a", "t", pw)
            pte = E.qsop("a", lambda a, gp=gsub % 2, base=base, n=n:
                         a.activation(
                out=pT[:, gp, 0:n, :, :],
                in_=trA[:, base * 1024:base * 1024 + n * 512]
                .rearrange("p (s t q) -> p s t q", t=4, q=128),
                func=AF.Copy))
            bank_user[base] = pte
            pte_ms[gsub] = pte

        def emit_av(sub):
            s, slots, gsub = sub
            ph2 = s % 2
            pte = pte_ms.pop(gsub)
            E.qwait("t", "a", pte)
            E.qwait("t", "vbuf%d" % ph2, vdma_m[s])
            for pos, k in enumerate(slots):
                tgt = psBC if pos == 0 else psD
                ob = obank_user.get(pos)
                if ob is not None:
                    E.qwait("t", "v", ob)
                av = E.qgroup("t", [
                    (lambda t, kt=kt, eh=eh, pos=pos, gp=gsub % 2, ph2=ph2,
                     tgt=tgt: t.matmul(
                        tgt[:, eh * 512:(eh + 1) * 512],
                        pT[:, gp, pos, kt, :],
                        vbuf[:, ph2, kt, eh * 512:(eh + 1) * 512],
                        start=(kt == 0), stop=(kt == 3)))
                    for kt in range(4) for eh in range(2)], "t")
                vbuf_av_last[ph2] = av
                pt_last[gsub % 2] = av

                E.qwait("v", "t", av)
                if s == 0:
                    ev = E.qsop("v", lambda v, k=k, tgt=tgt:
                                v.tensor_scalar_add(
                        oacc[:, k, :], tgt[:], 0.0))
                else:
                    ev = E.qsop("v", lambda v, k=k, tgt=tgt: v.tensor_tensor(
                        oacc[:, k, :], tgt[:], oacc[:, k, :], ALU.add))
                obank_user[pos] = ev
            if slots[0] == s:
                emit_finalize(s)

        issue_kbuf(0)
        issue_kbuf(1)
        issue_kbuf(2)
        issue_vbuf(0)
        issue_vbuf(1)
        issue_xrs(0)
        for s in range(NS):
            issue_kbuf(s + 3)
            active = list(range(s, NS))
            for i in range(0, len(active), 2):
                if pending:
                    emit_tr_pte(pending[0])
                emit_qk(s, active[i:i + 2])
                if len(pending) >= 2:
                    emit_av(pending.pop(0))
                if i == 0 and s >= 1:
                    issue_vbuf(s + 1)
        sub = pending.pop(0)
        emit_tr_pte(sub)
        emit_av(sub)

        E.qwait("pl", "out")

    with nc.Block() as blk:
        E.flush(blk)
    E.es.close()
    return nc


_NC_CACHE = {}


def _mk_inputs(X, wq, wk, wv, wg, wu, wd):
    ident = np.eye(128, dtype=np.float32).astype(BF)
    in_maps = []
    for core in range(8):
        b, j = core // 4, core % 4
        rows = np.arange(NS) * 4 + j
        xq = np.ascontiguousarray(
            X[b].reshape(64, 128, DM)[rows].reshape(QR, DM))
        c = np.arange(512)[None, :]
        r = np.arange(128)[:, None]
        mask = np.where(c <= 128 * j + r, 0.0, NEG).astype(np.float32)
        in_maps.append({
            "xk": X[b], "xq": xq,
            "wq": wq, "wk": wk, "wv": wv,
            "wg": wg, "wu": wu, "wd": wd,
            "mask": mask, "ident": ident,
        })
    return in_maps


def kernel(L0_attn_X, L0_attn_Wq, L0_attn_Wk, L0_attn_Wv,
           L0_ffn_w_gate, L0_ffn_w_up, L0_ffn_w_down):
    X = np.ascontiguousarray(np.asarray(L0_attn_X, dtype=np.float32))
    wq = np.ascontiguousarray(np.asarray(L0_attn_Wq, dtype=np.float32))
    wk = np.ascontiguousarray(np.asarray(L0_attn_Wk, dtype=np.float32))
    wv = np.ascontiguousarray(np.asarray(L0_attn_Wv, dtype=np.float32))
    wg = np.ascontiguousarray(np.asarray(L0_ffn_w_gate, dtype=np.int32))
    wu = np.ascontiguousarray(np.asarray(L0_ffn_w_up, dtype=np.int32))
    wd = np.ascontiguousarray(np.asarray(L0_ffn_w_down, dtype=np.int32))

    if "nc" not in _NC_CACHE:
        _NC_CACHE["nc"] = build()
    nc = _NC_CACHE["nc"]
    in_maps = _mk_inputs(X, wq, wk, wv, wg, wu, wd)
    res = run_bass_kernel_spmd(nc, in_maps, core_ids=list(range(8)))

    out = np.empty((2, S, DM), dtype=np.float32)
    for core in range(8):
        b, j = core // 4, core % 4
        rows = np.arange(NS) * 4 + j
        out[b].reshape(64, 128, DM)[rows] = \
            res.results[core]["out"].reshape(NS, 128, DM)
    return out
